# version 21
# baseline (speedup 1.0000x reference)
from contextlib import ExitStack

import numpy as np
import ml_dtypes

import concourse.bass as bass
import concourse.mybir as mybir
from concourse.bass_utils import run_bass_kernel_spmd

F32 = mybir.dt.float32
BF16 = mybir.dt.bfloat16
I32 = mybir.dt.int32
AF = mybir.ActivationFunctionType
ALU = mybir.AluOpType
AX = mybir.AxisListType

T = 2048
NT = 16
C = 1024
INNER = 4096
NKC = C // 128

DEBUG_TAPS = []

ENGINES = ("tensor", "vector", "scalar", "gpsimd", "sync")


class _Buf:
    __slots__ = ("name", "writer", "readers")

    def __init__(self, name):
        self.name = name
        self.writer = None
        self.readers = []


class _Op:
    __slots__ = ("idx", "engine", "dma", "fn", "deps", "needs_inc", "inc_index",
                 "tag", "_semkey", "amount")

    def __init__(self, idx, engine, dma, fn, deps, tag):
        self.idx = idx
        self.engine = engine
        self.dma = dma
        self.fn = fn
        self.deps = deps
        self.needs_inc = False
        self.inc_index = None
        self.tag = tag
        self._semkey = None
        self.amount = 16 if dma else 1

    @property
    def semkey(self):
        return self._semkey if self.dma else self.engine


class _Prog:

    DMA_SEM_POOL = 6

    def __init__(self):
        self.ops = []

    def op(self, engine, fn, reads=(), writes=(), dma=False, deps=(), tag="",
           force_inc=False, inc_amount=None, semkey=None):
        assert engine in ENGINES
        idx = len(self.ops)
        depset = set(deps)
        for r in reads:
            if r.writer is not None:
                depset.add(r.writer)
        for w in writes:
            if w.writer is not None:
                depset.add(w.writer)
            depset.update(w.readers)
        o = _Op(idx, engine, dma, fn, depset, tag)
        o.needs_inc = bool(force_inc)
        if inc_amount is not None:
            o.amount = inc_amount
        if semkey is not None:
            o._semkey = semkey
        self.ops.append(o)
        for w in writes:
            w.writer = idx
            w.readers = []
        for r in reads:
            r.readers.append(idx)
        return idx

    @staticmethod
    def fence(bufs):
        deps = set()
        for b in bufs:
            if b.writer is not None:
                deps.add(b.writer)
            deps.update(b.readers)
        return deps

    def _filter_deps(self):
        for o in self.ops:
            kept = set()
            for d in o.deps:
                pp = self.ops[d]
                if (not pp.dma) and (not o.dma) and pp.engine == o.engine == "tensor":
                    continue
                kept.add(d)
            o.deps = kept

    def _assign_incs(self):
        for o in self.ops:
            for d in o.deps:
                self.ops[d].needs_inc = True
        for o in self.ops:
            if o.dma:
                o.needs_inc = True
        rr, totals = {}, {}
        self.prev_use_val = {}
        for o in self.ops:
            if o.dma and o._semkey is None:
                qi = rr.get(o.engine, 0)
                rr[o.engine] = qi + 1
                o._semkey = f"q{o.engine}{qi % self.DMA_SEM_POOL}"
            if o.needs_inc:
                if o.dma:
                    self.prev_use_val[o.idx] = totals.get(o.semkey, 0)
                totals[o.semkey] = totals.get(o.semkey, 0) + o.amount
                o.inc_index = totals[o.semkey]
        self.sem_totals = totals

    def emit(self, nc, stack, final_engine="sync"):
        self._filter_deps()
        self._assign_incs()
        semkeys = sorted(self.sem_totals)
        sems = {k: stack.enter_context(nc.semaphore(f"sem_{k}")) for k in semkeys}
        by_engine = {e: [] for e in ENGINES}
        for o in self.ops:
            by_engine[o.engine].append(o)
        block = stack.enter_context(nc.Block())

        def make_runner(ops, is_final):
            def run(eng):
                covered = {}
                for o in ops:
                    waits = {}
                    for d in sorted(o.deps):
                        pp = self.ops[d]
                        if waits.get(pp.semkey, 0) < pp.inc_index:
                            waits[pp.semkey] = pp.inc_index
                    if o.dma and self.prev_use_val.get(o.idx, 0) > 0:
                        v = self.prev_use_val[o.idx]
                        if waits.get(o.semkey, 0) < v:
                            waits[o.semkey] = v
                    for k, v in sorted(waits.items()):
                        if covered.get(k, 0) >= v:
                            continue
                        eng.wait_ge(sems[k], v)
                        covered[k] = v
                    ins = o.fn(eng)
                    if o.needs_inc:
                        ins.then_inc(sems[o.semkey], o.amount)
                if is_final:
                    for k in semkeys:
                        eng.wait_ge(sems[k], self.sem_totals[k])
            return run

        for e in ENGINES:
            if not by_engine[e] and e != final_engine:
                continue
            getattr(block, e)(make_runner(by_engine[e], e == final_engine))


class _APWrap:
    def __init__(self, ap):
        self._ap = ap
    def __getitem__(self, k):
        return self._ap


def _b0(ap2d, reps):
    return bass.AP(tensor=ap2d.tensor, offset=ap2d.offset,
                   ap=list(ap2d.ap) + [[0, reps]])


def build_program():
    nc = bass.Bass("TRN2", target_bir_lowering=False)
    p = _Prog()
    stack = ExitStack()

    def din(name, shape, dt=F32):
        return nc.dram_tensor(name, shape, dt, kind="ExternalInput").ap()

    x_d = din("x", [T, C])
    xT_d = din("xT", [C, T])
    wqT_d = din("wqT", [C, C])
    wkT_d = din("wkT", [C, C])
    wvT_d = din("wvT", [C, C])
    wgT_d = din("wgT", [C, INNER])
    wuT_d = din("wuT", [C, INNER])
    wdT_d = din("wdT", [INNER, C])
    bqb_d = din("bqb", [128, C], BF16)
    bkb_d = din("bkb", [128, C], BF16)
    bvb_d = din("bvb", [128, C], BF16)
    gamb_d = din("gamb", [128, C])
    betb_d = din("betb", [128, C])
    cosA_d = din("cosA", [T, C], BF16)
    sinA_d = din("sinA", [T, C], BF16)
    idb_d = din("idb", [128, 128], BF16)
    idf_d = din("idf", [128, 128])
    onesb_d = din("onesb", [128, 1], BF16)
    onesrb_d = din("onesrb", [1, 128], BF16)
    ones1f_d = din("ones1f", [1, 128])
    ones1_d = din("ones1", [1, 128])
    onesf_d = din("onesf", [128, 1])
    mask2h_d = din("mask2h", [128, C], BF16)
    consts_d = din("consts", [128, 8])

    y_d = nc.dram_tensor("y", [T, C], F32, kind="ExternalOutput").ap()
    cc_in = nc.dram_tensor("cc_in", [129, C], F32).ap()
    cc_out = nc.dram_tensor("cc_out", [129, C], F32).ap()
    gu_d = nc.dram_tensor("gu_spill", [T, INNER], BF16).ap()
    wdt_d = nc.dram_tensor("wd_tern", [INNER, C], BF16).ap()

    taps = {}
    if "x1" in DEBUG_TAPS:
        taps["x1"] = nc.dram_tensor("tap_x1", [T, C], F32, kind="ExternalOutput").ap()
    if "attn" in DEBUG_TAPS:
        taps["attn"] = nc.dram_tensor("tap_attn", [T, C], F32, kind="ExternalOutput").ap()
    if "kv" in DEBUG_TAPS:
        taps["kv"] = nc.dram_tensor("tap_kv", [129, C], F32, kind="ExternalOutput").ap()

    sbp = lambda n, s, d: stack.enter_context(nc.sbuf_tensor(n, s, d))

    x_sb = sbp("x_sb", [128, NT * C], F32)
    s_f = [sbp(f"s_f{i}", [128, C], F32) for i in range(4)]
    s_g = [sbp(f"s_g{i}", [128, C], F32) for i in range(2)]
    land = [sbp("land0", [128, 2048], F32), sbp("land1", [128, 2048], F32)]
    consts = sbp("consts_sb", [128, 8], F32)
    idb = sbp("idb_sb", [128, 128], BF16)
    idf = sbp("idf_sb", [128, 128], F32)
    onesb = sbp("onesb_sb", [128, 1], BF16)
    onesrb = sbp("onesrb_sb", [1, 128], BF16)
    ones1f = sbp("ones1f_sb", [1, 128], F32)
    ones1 = sbp("ones1_sb", [1, 128], F32)
    onesf = sbp("onesf_sb", [128, 1], F32)
    wstat = sbp("wstat_sb", [128, 64], F32)
    wcols = sbp("wcols_sb", [128, 8], F32)
    rmax_g = sbp("rmax_g_sb", [128, NT * 8], F32)
    colt = sbp("colt_sb", [128, 24], F32)
    st12 = sbp("st12_sb", [128, 24], F32)
    st2 = sbp("st2_sb", [128, 8], F32)
    nwt = sbp("nwt_sb", [128, 8], F32)

    B = {}

    def gb(name):
        if name not in B:
            B[name] = _Buf(name)
        return B[name]

    def dma(queue, dst, src, reads=(), writes=(), tag="", deps=(), inc_amount=None):
        return p.op(queue, lambda e, d=dst, s=src: e.dma_start(out=d, in_=s),
                    reads=reads, writes=writes, dma=True, tag=tag, deps=deps,
                    force_inc=True, inc_amount=inc_amount)

    eps12 = consts[:, 0:1]
    eps5 = consts[:, 1:2]

    for dst, src, nme in (
        (consts[:, :], consts_d, "consts"), (idb[:, :], idb_d, "idb"),
        (idf[:, :], idf_d, "idf"),
        (onesb[:, :], onesb_d, "onesb"), (onesrb[:, :], onesrb_d, "onesrb"),
        (ones1f[:, :], ones1f_d, "ones1f"),
        (ones1[:, :], ones1_d, "ones1"), (onesf[:, :], onesf_d, "onesf"),
    ):
        dma("gpsimd", dst, src, writes=[gb(nme)], tag="ld_" + nme)

    xv = x_d.rearrange("(t p) c -> p t c", p=128)
    xbufs = [gb(f"x{t}") for t in range(NT)]
    bsf = [gb(f"s_f{i}") for i in range(4)]
    bsg = [gb(f"s_g{i}") for i in range(2)]
    landb = [gb("land0"), gb("land1")]

    ac = ExitStack()
    sba = lambda n, s, d: ac.enter_context(nc.sbuf_tensor(n, s, d))
    xT_sb = sba("xT_sb", [128, NKC * T], BF16)
    _ct = sba("cos0", [128, C], BF16)
    _st = sba("sin0", [128, C], BF16)
    costb = [_ct, _ct]
    sintb = [_st, _st]
    mrp = [sba(f"mrp{i}", [128, C], BF16) for i in range(2)]
    rin_b = sba("rin_b", [128, C], BF16)
    l1s = ExitStack()
    sb1 = lambda n, s, d: l1s.enter_context(nc.sbuf_tensor(n, s, d))
    kvfull = sb1("kvfull_sb", [128, C], F32)
    ksrow = sb1("ksrow_sb", [1, C], F32)
    wk_sb = sb1("wk_sb", [128, NKC * C], BF16)
    wv_sb = sb1("wv_sb", [128, NKC * C], BF16)
    bkb = sb1("bkb_sb", [128, C], BF16)
    bvb = sb1("bvb_sb", [128, C], BF16)
    _knr = sb1("knr0", [128, C], BF16)
    knr_b = [_knr, _knr]
    kr_b = [sb1(f"kr{i}", [128, C], BF16) for i in range(2)]
    vs_b = [sb1(f"vs{i}", [128, C], BF16) for i in range(2)]
    wa_land = land

    for dst, src, nme in (
        (bkb[:, :], bkb_d, "bkb"),
        (bvb[:, :], bvb_d, "bvb"),
    ):
        dma("gpsimd", dst, src, writes=[gb(nme)], tag="ld_" + nme)

    for kc in range(NKC):
        lb, lt = landb[kc % 2], land[kc % 2]
        dma("sync", lt[:, :], xT_d[128 * kc:128 * (kc + 1), :],
            writes=[lb], tag=f"ld_xT{kc}")
        p.op("vector", lambda e, kc=kc, lt=lt: e.tensor_copy(
            xT_sb[:, kc * T:(kc + 1) * T], lt[:, :]),
            reads=[lb], writes=[gb(f"xT{kc}")], tag=f"cast_xT{kc}")
    xTbufs = [gb(f"xT{kc}") for kc in range(NKC)]

    for wd_, wsb, nme in ((wkT_d, wk_sb, "wk"), (wvT_d, wv_sb, "wv")):
        wvw = wd_.rearrange("(k p) n -> p k n", p=128)
        for jj in range(4):
            lb, lt = landb[jj % 2], land[jj % 2]
            dma("sync", lt[:, :].rearrange("p (k n) -> p k n", n=C),
                wvw[:, 2 * jj:2 * jj + 2, :], writes=[lb], tag=f"ld_{nme}{jj}")
            p.op("vector", lambda e, jj=jj, lt=lt, wsb=wsb: e.tensor_copy(
                wsb[:, jj * 2 * C:(jj + 1) * 2 * C], lt[:, :]),
                reads=[lb], writes=[gb(f"{nme}{jj}")], tag=f"cast_{nme}{jj}")
    wkb = [gb(f"wk{j}") for j in range(4)]
    wvb = [gb(f"wv{j}") for j in range(4)]

    for t in range(NT):
        dma("sync", x_sb[:, t * C:(t + 1) * C], xv[:, t, :],
            writes=[xbufs[t]], tag=f"ld_x{t}")

    col_of = {0: 0, 1: 16, 2: 32}
    ccount = {0: 0, 1: 0, 2: 0}
    wst_bufs = {0: [], 1: [], 2: []}
    wabs_chunks = []
    for mi, wd_ in enumerate((wgT_d, wuT_d, wdT_d)):
        w3 = wd_.rearrange("(k p) n -> p k n", p=128)
        ncol = wd_.shape[1]
        if ncol >= 2048:
            for k in range(wd_.shape[0] // 128):
                for j in range(ncol // 2048):
                    wabs_chunks.append((mi, w3[:, k:k + 1, j * 2048:(j + 1) * 2048]))
        else:
            for k in range(0, wd_.shape[0] // 128, 2):
                wabs_chunks.append((mi, w3[:, k:k + 2, :]))
    def emit_abs_chunk(ci):
        mi, ch = wabs_chunks[ci]
        lb, lt = landb[ci % 2], wa_land[ci % 2]
        dma("gpsimd", lt[:, :].rearrange("p (k n) -> p k n", n=ch.shape[2]), ch,
            writes=[lb], tag=f"wabs_ld{ci}")
        col = col_of[mi] + ccount[mi]
        ccount[mi] += 1
        wb = gb(f"wst_{mi}_{col}")
        wst_bufs[mi].append(wb)
        p.op("scalar", lambda e, lt=lt, col=col: e.activation(
            lt[:, :], lt[:, :], AF.Abs, accum_out=wstat[:, col:col + 1]),
            reads=[lb], writes=[wb, lb], tag=f"wabs_red{ci}")

    pa = ExitStack()
    pj0 = pa.enter_context(nc.psum_tensor("pj0", [128, C], F32))
    pj1 = pa.enter_context(nc.psum_tensor("pj1", [128, C], F32))
    pA = ExitStack()
    pkv = pA.enter_context(nc.psum_tensor("pkv", [128, C], F32))
    pks = pA.enter_context(nc.psum_tensor("pks", [1, C], F32))
    bpj = [gb("pj0"), gb("pj1")]
    bpkv, bpks = gb("pkv"), gb("pks")

    def bias_mm(psum, bps, biast, biasbuf, tag):
        for n in range(2):
            p.op("tensor", lambda e, n=n, psum=psum, biast=biast: e.matmul(
                psum[:, 512 * n:512 * (n + 1)], onesrb[:, :],
                biast[0:1, 512 * n:512 * (n + 1)], start=False, stop=True,
                skip_group_check=True),
                reads=[gb("onesrb"), biasbuf], writes=[bps], tag=tag + f"_{n}")

    def proj_mm(psum, bps, wsb, wbufs, t, tag):
        for kc in range(NKC):
            for n in range(2):
                p.op("tensor", lambda e, kc=kc, n=n, psum=psum, wsb=wsb, t=t:
                     e.matmul(psum[:, 512 * n:512 * (n + 1)],
                              xT_sb[:, kc * T + 128 * t:kc * T + 128 * t + 128],
                              wsb[:, kc * C + 512 * n:kc * C + 512 * (n + 1)],
                              start=(kc == 0), stop=(kc == NKC - 1),
                              skip_group_check=True),
                     reads=[xTbufs[kc], wbufs[kc // 2]], writes=[bps],
                     tag=f"{tag}_mm{kc}_{n}")

    def rope(src_ap, srcbufs, par, dst, dstbuf, tag, precast=False):
        if precast:
            p.op("vector", lambda e, src_ap=src_ap: e.tensor_copy(rin_b[:, :], src_ap),
                 reads=list(srcbufs), writes=[gb("rin_b")], tag=tag + "_pc")
            src_ap = rin_b[:, :]
            srcbufs = [gb("rin_b")]
        s3 = src_ap.rearrange("p (h half j) -> p h half j", half=2, j=32)
        d3 = dst[:, :].rearrange("p (h half j) -> p h half j", half=2, j=32)
        m0 = mrp[0][:, :]
        m1 = mrp[1][:, :]
        m0s = m0.rearrange("p (h half j) -> p h half j", half=2, j=32)
        m1s = m1.rearrange("p (h half j) -> p h half j", half=2, j=32)
        bm = [gb("mrp0"), gb("mrp1")]
        tb = [gb("cos0"), gb("sin0")]
        p.op("vector", lambda e, m0=m0, src_ap=src_ap: e.tensor_tensor(
            m0, src_ap, costb[0][:, :], ALU.mult),
            reads=list(srcbufs) + [tb[0]], writes=[bm[0]], tag=tag + "_m0")
        p.op("vector", lambda e, m1=m1, src_ap=src_ap: e.tensor_tensor(
            m1, src_ap, sintb[0][:, :], ALU.mult),
            reads=list(srcbufs) + [tb[1]], writes=[bm[1]], tag=tag + "_m1")
        p.op("vector", lambda e, d3=d3, m0s=m0s, m1s=m1s: e.tensor_tensor(
            d3[:, :, 0, :], m0s[:, :, 0, :], m1s[:, :, 1, :], ALU.subtract),
            reads=[bm[0], bm[1]], writes=[dstbuf], tag=tag + "_sub")
        p.op("vector", lambda e, d3=d3, m0s=m0s, m1s=m1s: e.tensor_tensor(
            d3[:, :, 1, :], m1s[:, :, 0, :], m0s[:, :, 1, :], ALU.add),
            reads=[bm[0], bm[1]], writes=[dstbuf], tag=tag + "_add")

    cosv = cosA_d.rearrange("(t p) i -> p t i", p=128)
    sinv = sinA_d.rearrange("(t p) i -> p t i", p=128)

    def load_tables(t, par):
        dma("gpsimd", costb[0][:, :], cosv[:, t, :],
            writes=[gb("cos0")], tag=f"ld_cos{t}")
        dma("gpsimd", sintb[0][:, :], sinv[:, t, :],
            writes=[gb("sin0")], tag=f"ld_sin{t}")

    for t in range(NT):
        par = t % 2
        load_tables(t, par)
        proj_mm(pj0, bpj[0], wk_sb, wkb, t, f"k{t}")
        bias_mm(pj0, bpj[0], bkb, gb("bkb"), f"kbias{t}")
        p.op("scalar", lambda e, par=par: e.activation(
            s_g[par][:, :], pj0[:, :], AF.Gelu),
            reads=[bpj[0]], writes=[bsg[par]], tag=f"kgelu{t}")
        p.op("vector", lambda e, par=par: e.tensor_scalar_add(
            knr_b[par][:, :], s_g[par][:, :], 0.21),
            reads=[bsg[par]], writes=[gb("knr0")], tag=f"k021_{t}")
        for n in range(2):
            p.op("tensor", lambda e, n=n, t=t, par=par: e.matmul(
                pks[0:1, 512 * n:512 * (n + 1)], onesb[:, :],
                knr_b[par][:, 512 * n:512 * (n + 1)],
                start=(t == 0), stop=(t == NT - 1), skip_group_check=True),
                reads=[gb("onesb"), gb("knr0")], writes=[bpks],
                tag=f"ksum{t}_{n}")
        rope(knr_b[par][:, :], [gb("knr0")], par, kr_b[par],
             gb(f"kr{par}"), f"ropek{t}")
        proj_mm(pj1, bpj[1], wv_sb, wvb, t, f"v{t}")
        bias_mm(pj1, bpj[1], bvb, gb("bvb"), f"vbias{t}")
        p.op("scalar", lambda e, par=par: e.activation(
            vs_b[par][:, :], pj1[:, :], AF.Copy, scale=2.0 ** -12),
            reads=[bpj[1]], writes=[gb(f"vs{par}")], tag=f"vsc{t}")
        for c2 in range(NKC):
            p.op("tensor", lambda e, c2=c2, t=t, par=par: e.matmul(
                pkv[:, 128 * c2:128 * (c2 + 1)],
                kr_b[par][:, 128 * c2:128 * (c2 + 1)],
                vs_b[par][:, 128 * c2:128 * (c2 + 1)],
                start=(t == 0), stop=(t == NT - 1), skip_group_check=True),
                reads=[gb(f"kr{par}"), gb(f"vs{par}")], writes=[bpkv],
                tag=f"kv{t}_{c2}")
        for ci in range(3 * t, 3 * t + 3):
            emit_abs_chunk(ci)

    p.op("vector", lambda e: e.tensor_copy(kvfull[:, :], pkv[:, :]),
         reads=[bpkv], writes=[gb("kvfull")], tag="ev_kv")
    p.op("vector", lambda e: e.tensor_copy(ksrow[:, :], pks[:, :]),
         reads=[bpks], writes=[gb("ksrow")], tag="ev_ks")
    dma("gpsimd", cc_in[0:128, :], kvfull[:, :], reads=[gb("kvfull")],
        writes=[gb("ccin")], tag="cc_st_kv")
    dma("gpsimd", cc_in[128:129, :], ksrow[:, :], reads=[gb("ksrow")],
        writes=[gb("ccin")], tag="cc_st_ks")
    p.op("gpsimd", lambda e: e.collective_compute(
        "AllReduce", ALU.add, replica_groups=[[0, 1], [2, 3], [4, 5], [6, 7]],
        ins=[cc_in], outs=[cc_out]),
        reads=[gb("ccin")], writes=[gb("ccout")], dma=True, inc_amount=1,
        tag="cc", semkey="qcc")
    if "kv" in DEBUG_TAPS:
        dma("gpsimd", taps["kv"], cc_out, reads=[gb("ccout")],
            writes=[gb("tapkv")], tag="tap_kv")

    fence_l1 = _Prog.fence([gb("kvfull"), gb("ksrow")] +
                           [gb(f"wk{j}") for j in range(4)] +
                           [gb(f"wv{j}") for j in range(4)] +
                           [gb("bkb"), gb("bvb")] +
                           [gb("knr0")] +
                           [gb(f"kr{i}") for i in range(2)] +
                           [gb(f"vs{i}") for i in range(2)])
    l1s.close()
    l2s = ExitStack()
    sb2 = lambda n, s, d: l2s.enter_context(nc.sbuf_tensor(n, s, d))
    wq_sb = sb2("wq_sb", [128, NKC * C], BF16)
    bqb = sb2("bqb_sb", [128, C], BF16)
    gamb = sb2("gamb_sb", [128, C], F32)
    betb = sb2("betb_sb", [128, C], F32)
    mask2h = sb2("mask2h_sb", [128, C], BF16)
    kvf2 = sb2("kvf2_sb", [128, C], F32)
    ksr2 = sb2("ksr2_sb", [1, C], F32)
    kv_b = sb2("kv_b_sb", [128, C], BF16)
    kmean = sb2("kmean_sb", [128, C], F32)
    qr_b = [sb2(f"qr{i}", [128, C], BF16) for i in range(2)]
    qrT_s = [sb2(f"qrT{i}", [128, C], BF16) for i in range(2)]
    z_s = [sb2(f"z{i}", [128, 16], F32) for i in range(2)]
    for dst, src, nme in ((bqb[:, :], bqb_d, "bqb"),
                          (gamb[:, :], gamb_d, "gamb"),
                          (betb[:, :], betb_d, "betb"),
                          (mask2h[:, :], mask2h_d, "mask2h")):
        dma("gpsimd", dst, src, writes=[gb(nme)], tag="ld_" + nme,
            deps=fence_l1)
    dma("gpsimd", kvf2[:, :], cc_out[0:128, :], reads=[gb("ccout")],
        writes=[gb("kvf2")], tag="cc_ld_kv", deps=fence_l1)
    dma("gpsimd", ksr2[:, :], cc_out[128:129, :], reads=[gb("ccout")],
        writes=[gb("ksr2")], tag="cc_ld_ks", deps=fence_l1)
    p.op("vector", lambda e: e.tensor_tensor(kv_b[:, :], kvf2[:, :],
                                             mask2h[:, :], ALU.mult),
         reads=[gb("kvf2"), gb("mask2h")], writes=[gb("kv_b")], tag="kv_mask")
    for n in range(2):
        p.op("tensor", lambda e, n=n: e.matmul(
            pj0[:, 512 * n:512 * (n + 1)], ones1f[:, :],
            ksr2[0:1, 512 * n:512 * (n + 1)], start=True, stop=True),
            reads=[gb("ones1f"), gb("ksr2")], writes=[bpj[0]], tag=f"kmean{n}")
    p.op("vector", lambda e: e.tensor_copy(kmean[:, :], pj0[:, :]),
         reads=[bpj[0]], writes=[gb("kmean")], tag="ev_kmean")
    wvw = wqT_d.rearrange("(k p) n -> p k n", p=128)
    for jj in range(4):
        lb, lt = landb[jj % 2], land[jj % 2]
        dma("sync", lt[:, :].rearrange("p (k n) -> p k n", n=C),
            wvw[:, 2 * jj:2 * jj + 2, :], writes=[lb], tag=f"ld_wq{jj}",
            deps=fence_l1)
        p.op("vector", lambda e, jj=jj, lt=lt: e.tensor_copy(
            wq_sb[:, jj * 2 * C:(jj + 1) * 2 * C], lt[:, :]),
            reads=[lb], writes=[gb(f"wq{jj}")], tag=f"cast_wq{jj}")
    wqb = [gb(f"wq{j}") for j in range(4)]

    fence_b = _Prog.fence([bpkv, bpks])
    pA.close()
    pC = ExitStack()
    pqT = pC.enter_context(nc.psum_tensor("pqT", [128, C], BF16))
    pout = pC.enter_context(nc.psum_tensor("pout", [128, C], F32))
    bpqT, bpout = gb("pqT"), gb("pout")

    def _rsqrt_newton(vcol, vbuf, ocol, obuf, eps, nb, tag):
        ve = nwt[:, nb:nb + 1]
        y = nwt[:, nb + 1:nb + 2]
        aa = nwt[:, nb + 2:nb + 3]
        cc = nwt[:, nb + 3:nb + 4]
        bnw = gb(f"nwt{nb}")
        p.op("vector", lambda e: e.tensor_scalar_add(ve, vcol, eps),
             reads=[vbuf], writes=[bnw], tag=tag + "_ve")
        p.op("vector", lambda e: e.tensor_scalar(
            y.bitcast(I32), ve.bitcast(I32), 1, None, ALU.arith_shift_right),
            reads=[bnw], writes=[bnw], tag=tag + "_sh")
        p.op("vector", lambda e: e.tensor_scalar(
            y.bitcast(I32), y.bitcast(I32), -1, 0x5F3759DF, ALU.mult, ALU.add),
            reads=[bnw], writes=[bnw], tag=tag + "_sd")
        for it in range(3):
            p.op("vector", lambda e: e.tensor_tensor(aa, y, y, ALU.mult),
                 reads=[bnw], writes=[bnw], tag=tag + f"_a{it}")
            p.op("vector", lambda e: e.tensor_tensor(aa, aa, ve, ALU.mult),
                 reads=[bnw], writes=[bnw], tag=tag + f"_b{it}")
            p.op("vector", lambda e: e.tensor_scalar(cc, aa, -0.5, 1.5,
                                                     ALU.mult, ALU.add),
                 reads=[bnw], writes=[bnw], tag=tag + f"_c{it}")
            if it < 2:
                p.op("vector", lambda e: e.tensor_tensor(y, cc, y, ALU.mult),
                     reads=[bnw], writes=[bnw], tag=tag + f"_y{it}")
            else:
                p.op("vector", lambda e, ocol=ocol: e.tensor_tensor(
                    ocol, cc, y, ALU.mult),
                    reads=[bnw], writes=[obuf], tag=tag + f"_y{it}")

    def ln(src, srcbuf, dstslot, dstbuf, epscol, epsbuf, t, nme):
        o = 12 * (t % 2)
        o2 = 4 * (t % 2)
        p.op("vector", lambda e, src=src, o=o: e.bn_stats(
            st12[:, o:o + 6], src[:, 0:512]),
            reads=[srcbuf], writes=[gb("st12")], tag=f"{nme}_bn0_{t}")
        p.op("vector", lambda e, src=src, o=o: e.bn_stats(
            st12[:, o + 6:o + 12], src[:, 512:1024]),
            reads=[srcbuf], writes=[gb("st12")], tag=f"{nme}_bn1_{t}")
        p.op("vector", lambda e, o=o, o2=o2: e.bn_aggr(
            st2[:, o2:o2 + 2], st12[:, o:o + 12]),
            reads=[gb("st12")], writes=[gb("st2")], tag=f"{nme}_ag_{t}")
        p.op("scalar", lambda e, o2=o2, epscol=epscol: e.activation(
            colt[:, o2:o2 + 1], st2[:, o2 + 1:o2 + 2], AF.Sqrt, bias=epscol),
            reads=[gb("st2"), epsbuf], writes=[gb("colt")], tag=f"{nme}_sq_{t}")
        p.op("vector", lambda e, o2=o2: e.reciprocal(
            colt[:, o2 + 1:o2 + 2], colt[:, o2:o2 + 1]),
            reads=[gb("colt")], writes=[gb("colt")], tag=f"{nme}_rc_{t}")
        p.op("vector", lambda e, src=src, dstslot=dstslot, o2=o2: e.tensor_scalar(
            dstslot[:, :], src[:, :], st2[:, o2:o2 + 1],
            colt[:, o2 + 1:o2 + 2], ALU.subtract, ALU.mult),
            reads=[srcbuf, gb("st2"), gb("colt")], writes=[dstbuf],
            tag=f"{nme}_nm_{t}")

    pouts = [pj1, pout]
    bpouts = [bpj[1], bpout]

    def l2_s1(t):
        par = t % 2
        load_tables(t, par)
        proj_mm(pj0, bpj[0], wq_sb, wqb, t, f"q{t}")
        bias_mm(pj0, bpj[0], bqb, gb("bqb"), f"qbias{t}")
        p.op("scalar", lambda e, par=par: e.activation(
            s_g[par][:, :], pj0[:, :], AF.Gelu),
            reads=[bpj[0]], writes=[bsg[par]], tag=f"qgelu{t}")
        p.op("vector", lambda e, par=par: e.tensor_scalar_add(
            s_f[2 + par][:, :], s_g[par][:, :], 0.21),
            reads=[bsg[par]], writes=[bsf[2 + par]], tag=f"q021_{t}")
        rope(s_f[2 + par][:, :], [bsf[2 + par]], par, qr_b[par],
             gb(f"qr{par}"), f"ropeq{t}", precast=True)
        for kc in range(NKC):
            p.op("tensor", lambda e, kc=kc, par=par: e.transpose(
                pqT[:, 128 * kc:128 * (kc + 1)],
                qr_b[par][:, 128 * kc:128 * (kc + 1)], idb[:, :]),
                reads=[gb(f"qr{par}"), gb("idb")], writes=[bpqT],
                tag=f"qT{t}_{kc}", deps=fence_b if t == 0 and kc == 0 else ())
        p.op("scalar", lambda e, par=par: e.copy(
            qrT_s[par][:, :], pqT[:, :]),
            reads=[bpqT], writes=[gb(f"qrT{par}")], tag=f"ev_qT{t}")
        for c2 in range(NKC):
            p.op("tensor", lambda e, c2=c2, par=par: e.matmul(
                pouts[par][:, 128 * c2:128 * (c2 + 1)],
                qrT_s[par][:, 128 * c2:128 * (c2 + 1)],
                kv_b[:, 128 * c2:128 * (c2 + 1)], start=True, stop=True),
                reads=[gb(f"qrT{par}"), gb("kv_b")], writes=[bpouts[par]],
                tag=f"omm{t}_{c2}", deps=fence_b if t == 0 and c2 == 0 else ())

    def l2_s2(t):
        par = t % 2
        p.op("vector", lambda e, par=par: e.tensor_tensor(
            s_g[par][:, :], s_f[2 + par][:, :], kmean[:, :], ALU.mult),
            reads=[bsf[2 + par], gb("kmean")], writes=[bsg[par]], tag=f"zprod{t}")
        p.op("vector", lambda e, par=par: e.reduce_sum(
            z_s[par][:, 0:16],
            s_g[par][:, :].rearrange("p (h d) -> p h d", d=64), AX.X),
            reads=[bsg[par]], writes=[gb(f"z{par}")], tag=f"zred{t}")
        p.op("vector", lambda e, par=par: e.tensor_scalar_add(
            z_s[par][:, 0:16], z_s[par][:, 0:16], 1e-6),
            reads=[gb(f"z{par}")], writes=[gb(f"z{par}")], tag=f"zeps{t}")
        p.op("vector", lambda e, par=par: e.reciprocal(
            z_s[par][:, 0:16], z_s[par][:, 0:16]),
            reads=[gb(f"z{par}")], writes=[gb(f"z{par}")], tag=f"zrec{t}")
        p.op("vector", lambda e, par=par: e.tensor_tensor(
            s_f[par][:, :].rearrange("p (h d) -> p h d", d=64),
            pouts[par][:, :].rearrange("p (h d) -> p h d", d=64),
            _b0(z_s[par][:, 0:16], 64), ALU.mult),
            reads=[bpouts[par], gb(f"z{par}")], writes=[bsf[par]], tag=f"oz{t}")
        if "attn" in DEBUG_TAPS:
            dma("sync", taps["attn"].rearrange("(t p) c -> p t c", p=128)[:, t, :],
                s_f[par][:, :], reads=[bsf[par]], writes=[gb("tapattn")],
                tag=f"tap_attn{t}")
        p.op("vector", lambda e, par=par, t=t: e.tensor_tensor(
            s_g[par][:, :], s_f[par][:, :], x_sb[:, t * C:(t + 1) * C], ALU.add),
            reads=[bsf[par], xbufs[t]], writes=[bsg[par]], tag=f"res1_{t}")
        ln(s_g[par], bsg[par], s_f[par], bsf[par], eps12, gb("consts"), t, "ln1")
        p.op("vector", lambda e, par=par, t=t: e.tensor_tensor(
            s_f[2 + par][:, :], x_sb[:, t * C:(t + 1) * C], betb[:, :], ALU.add),
            reads=[xbufs[t], gb("betb")], writes=[bsf[2 + par]], tag=f"xb{t}")
        p.op("vector", lambda e, par=par: e.tensor_tensor(
            s_f[par][:, :], s_f[par][:, :], gamb[:, :], ALU.mult),
            reads=[bsf[par], gb("gamb")], writes=[bsf[par]], tag=f"gam{t}")
        p.op("vector", lambda e, par=par: e.tensor_tensor(
            s_g[par][:, :], s_f[par][:, :], s_f[2 + par][:, :], ALU.add),
            reads=[bsf[par], bsf[2 + par]], writes=[bsg[par]], tag=f"res2_{t}")
        ln2dst = bass.AP(tensor=x_sb[:, :].tensor, offset=t * C,
                         ap=[[NT * C, 128], [1, C]])
        ln(s_g[par], bsg[par], _APWrap(ln2dst), xbufs[t], eps5, gb("consts"), t, "ln2")

    l2_s1(0)
    for t in range(NT):
        if t + 1 < NT:
            l2_s1(t + 1)
        l2_s2(t)

    if "x1" in DEBUG_TAPS:
        for t in range(NT):
            dma("sync", taps["x1"].rearrange("(t p) c -> p t c", p=128)[:, t, :],
                x_sb[:, t * C:(t + 1) * C], reads=[xbufs[t]],
                writes=[gb("tapx1")], tag=f"tap_x1_{t}")

    attn_bufs = (xTbufs + wkb + wvb + wqb +
                 [gb(n) for n in ("bqb", "bkb", "bvb", "gamb", "betb",
                                  "mask2h", "kvfull", "kv_b", "ksrow",
                                  "kmean", "wa0", "wa1")] +
                 [gb("cos0"), gb("sin0")] +
                 [gb(f"knr{i}") for i in range(2)] +
                 [gb(f"kr{i}") for i in range(2)] +
                 [gb(f"vs{i}") for i in range(2)] +
                 [gb(f"qr{par}"), gb(f"qrT{par}")] +
                 [gb(f"z{i}") for i in range(2)] +
                 [gb(f"mrp{i}") for i in range(2)])
    fence_ac = _Prog.fence(attn_bufs + [bpj[0], bpj[1], bpqT, bpout])
    l2s.close()
    ac.close()
    pC.close()
    pa.close()

    ti32 = stack.enter_context(nc.sbuf_tensor("ti32", [128, 2048], I32))
    ff = ExitStack()
    sbf = lambda n, s, d: ff.enter_context(nc.sbuf_tensor(n, s, d))
    xiT_sb = sbf("xiT_sb", [128, NKC * T], BF16)
    terng = [sbf(f"terng{i}", [128, INNER], BF16) for i in range(2)]
    ternu = [sbf(f"ternu{i}", [128, INNER], BF16) for i in range(2)]
    ternd = sbf("ternd", [128, 2048], BF16)
    gee = [sbf(f"gee{i}", [128, 512], F32) for i in range(2)]
    uee = [sbf(f"uee{i}", [128, 512], F32) for i in range(2)]
    gustage = [sbf(f"gust{i}", [128, 512], BF16) for i in range(2)]
    sax = sbf("sax", [128, NT], F32)
    rsax = sbf("rsax", [128, NT], F32)
    grs = sbf("grs", [128, NT], F32)
    urs = sbf("urs", [128, NT], F32)
    xib = sbf("xib", [128, C], BF16)

    pf = ExitStack()
    pg = [pf.enter_context(nc.psum_tensor(f"pg{i}", [128, 512], F32)) for i in range(2)]
    pu = [pf.enter_context(nc.psum_tensor(f"pu{i}", [128, 512], F32)) for i in range(2)]
    pxi = pf.enter_context(nc.psum_tensor("pxi", [128, C], BF16))
    psc = pf.enter_context(nc.psum_tensor("psc", [128, 512], F32))
    bpg = [gb("pg0"), gb("pg1")]
    bpu = [gb("pu0"), gb("pu1")]
    bpxi, bpsc = gb("pxi"), gb("psc")

    for mi in range(3):
        cw = col_of[mi]
        nch = ccount[mi]
        p.op("vector", lambda e, cw=cw, nch=nch, mi=mi: e.reduce_sum(
            wstat[:, 48 + mi:49 + mi], wstat[:, cw:cw + nch], AX.X),
            reads=wst_bufs[mi], writes=[gb(f"wrs{mi}")], tag=f"wrs{mi}")
        p.op("tensor", lambda e, mi=mi: e.matmul(
            psc[0:1, mi:mi + 1], onesf[:, :], wstat[:, 48 + mi:49 + mi],
            start=True, stop=True),
            reads=[gb("onesf"), gb(f"wrs{mi}")], writes=[bpsc],
            tag=f"wS{mi}", deps=fence_ac if mi == 0 else ())
    p.op("vector", lambda e: e.tensor_scalar(
        wstat[0:1, 52:55], psc[0:1, 0:3], 1.0 / (C * INNER), 1e-5,
        ALU.mult, ALU.max),
        reads=[bpsc], writes=[gb("wmu")], tag="wmean")
    p.op("vector", lambda e: e.reciprocal(wstat[0:1, 56:59], wstat[0:1, 52:55]),
         reads=[gb("wmu")], writes=[gb("wsw")], tag="wsw")
    p.op("tensor", lambda e: e.matmul(psc[:, 8:11], ones1[:, :],
                                      wstat[0:1, 52:55], start=True, stop=True),
         reads=[gb("ones1"), gb("wmu")], writes=[bpsc], tag="bc_m")
    p.op("tensor", lambda e: e.matmul(psc[:, 11:14], ones1[:, :],
                                      wstat[0:1, 56:59], start=True, stop=True),
         reads=[gb("ones1"), gb("wsw")], writes=[bpsc], tag="bc_sw")
    p.op("vector", lambda e: e.tensor_copy(wcols[:, 0:6], psc[:, 8:14]),
         reads=[bpsc], writes=[gb("wcols")], tag="ev_wcols")
    swcol = {0: wcols[:, 3:4], 1: wcols[:, 4:5], 2: wcols[:, 5:6]}

    for t in range(NT):
        par = t % 2
        xt = x_sb[:, t * C:(t + 1) * C]
        p.op("vector", lambda e, xt=xt, par=par: e.reduce_max(
            colt[:, 20 + par:21 + par], xt, AX.X, apply_absolute_value=True),
            reads=[xbufs[t]], writes=[gb(f"coltq{par}")], tag=f"xmax{t}",
            deps=fence_ac if t == 0 else ())
        p.op("vector", lambda e, t=t, par=par: e.tensor_scalar(
            rsax[:, t:t + 1], colt[:, 20 + par:21 + par], 1e-5, 1.0 / 127.0,
            ALU.max, ALU.mult),
            reads=[gb(f"coltq{par}")], writes=[gb("rsax")], tag=f"rsa{t}")
        p.op("vector", lambda e, t=t: e.reciprocal(
            sax[:, t:t + 1], rsax[:, t:t + 1]),
            reads=[gb("rsax")], writes=[gb("sax")], tag=f"sa{t}")
        p.op("vector", lambda e, xt=xt, t=t: e.tensor_scalar_mul(
            ti32[:, 0:C], xt, sax[:, t:t + 1]),
            reads=[xbufs[t], gb("sax")], writes=[gb("ti32")], tag=f"xii{t}",
            deps=fence_ac if t == 0 else ())
        p.op("scalar", lambda e: e.copy(xib[:, :], ti32[:, 0:C]),
             reads=[gb("ti32")], writes=[gb("xib")], tag=f"xib{t}")
        for kc in range(NKC):
            p.op("tensor", lambda e, kc=kc: e.transpose(
                pxi[:, 128 * kc:128 * (kc + 1)],
                xib[:, 128 * kc:128 * (kc + 1)], idb[:, :]),
                reads=[gb("xib"), gb("idb")], writes=[bpxi], tag=f"xiT{t}_{kc}",
                deps=fence_ac if t == 0 and kc == 0 else ())
        p.op("scalar", lambda e, t=t: e.copy(
            xiT_sb[:, :].rearrange("p (k tt) -> p k tt", tt=T)[:, :, 128 * t:128 * (t + 1)],
            pxi[:, :].rearrange("p (k tt) -> p k tt", tt=128)),
            reads=[bpxi], writes=[gb(f"xiTc{t}")], tag=f"ev_xiT{t}")
    xiTbufs = [gb(f"xiTc{t}") for t in range(NT)]
    p.op("vector", lambda e: e.tensor_scalar_mul(grs[:, :], rsax[:, :],
                                                 wcols[:, 0:1]),
         reads=[gb("rsax"), gb("wcols")], writes=[gb("grs")], tag="grs")
    p.op("vector", lambda e: e.tensor_scalar_mul(urs[:, :], rsax[:, :],
                                                 wcols[:, 1:2]),
         reads=[gb("rsax"), gb("wcols")], writes=[gb("urs")], tag="urs")

    w3g = wgT_d.rearrange("(k p) n -> p k n", p=128)
    w3u = wuT_d.rearrange("(k p) n -> p k n", p=128)
    guv = gu_d.rearrange("(t p) n -> p t n", p=128)
    gurow = [gb(f"gurow{t}") for t in range(NT)]

    w3d = wdT_d.rearrange("(k p) n -> p k n", p=128)
    wdt3 = wdt_d.rearrange("(k p) n -> p k n", p=128)

    def emit_wd_chunk(j):
        for half in range(2):
            lb, lt = landb[half], land[half]
            dma("sync", lt[:, :].rearrange("p (k n) -> p k n", n=C),
                w3d[:, 4 * j + 2 * half:4 * j + 2 * half + 2, :],
                writes=[lb], tag=f"wdld{j}_{half}")
            p.op("vector", lambda e, lt=lt: e.tensor_scalar_mul(
                ti32[:, :], lt[:, :], swcol[2]),
                reads=[lb, gb("wcols")], writes=[gb("ti32")],
                tag=f"wdsc{j}_{half}")
            p.op("scalar", lambda e: e.copy(ternd[:, :], ti32[:, :]),
                 reads=[gb("ti32")], writes=[gb("ternd")], tag=f"wdb{j}_{half}")
            p.op("vector", lambda e: e.tensor_scalar(
                ternd[:, :], ternd[:, :], -1.0, 1.0, ALU.max, ALU.min),
                reads=[gb("ternd")], writes=[gb("ternd")], tag=f"wdcl{j}_{half}")
            dma("gpsimd", wdt3[:, 4 * j + 2 * half:4 * j + 2 * half + 2, :],
                ternd[:, :].rearrange("p (k n) -> p k n", n=C),
                reads=[gb("ternd")], writes=[gb("wdtd")], tag=f"wdst{j}_{half}")

    def emit_tern_chunk(c):
        wpar = c % 2
        for (w3, tern, nme, mi) in ((w3g, terng, "g", 0), (w3u, ternu, "u", 1)):
            tb = gb(f"tern{nme}{wpar}")
            for half in range(2):
                lb, lt = landb[half], land[half]
                dma("sync", lt[:, :].rearrange("p (k n) -> p k n", n=512),
                    w3[:, 4 * half:4 * half + 4, 512 * c:512 * (c + 1)],
                    writes=[lb], tag=f"w{nme}ld{c}_{half}")
                p.op("vector", lambda e, lt=lt, mi=mi: e.tensor_scalar_mul(
                    ti32[:, :], lt[:, :], swcol[mi]),
                    reads=[lb, gb("wcols")], writes=[gb("ti32")],
                    tag=f"w{nme}sc{c}_{half}")
                p.op("scalar", lambda e, tern=tern, wpar=wpar, half=half: e.copy(
                    tern[wpar][:, 2048 * half:2048 * (half + 1)], ti32[:, :]),
                    reads=[gb("ti32")], writes=[tb], tag=f"w{nme}b{c}_{half}")
            p.op("vector", lambda e, tern=tern, wpar=wpar: e.tensor_scalar(
                tern[wpar][:, :], tern[wpar][:, :], -1.0, 1.0, ALU.max, ALU.min),
                reads=[tb], writes=[tb], tag=f"w{nme}cl{c}")

    emit_tern_chunk(0)
    for c in range(8):
        wpar = c % 2
        for t in range(NT):
            par = t % 2
            for kc in range(NKC):
                p.op("tensor", lambda e, kc=kc, t=t, wpar=wpar, par=par: e.matmul(
                    pg[par][:, :],
                    xiT_sb[:, kc * T + 128 * t:kc * T + 128 * t + 128],
                    terng[wpar][:, kc * 512:(kc + 1) * 512],
                    start=(kc == 0), stop=(kc == NKC - 1), skip_group_check=True),
                    reads=[xiTbufs[t], gb(f"terng{wpar}")], writes=[bpg[par]],
                    tag=f"gmm{c}_{t}_{kc}",
                    deps=fence_ac if c == 0 and t == 0 and kc == 0 else ())
            for kc in range(NKC):
                p.op("tensor", lambda e, kc=kc, t=t, wpar=wpar, par=par: e.matmul(
                    pu[par][:, :],
                    xiT_sb[:, kc * T + 128 * t:kc * T + 128 * t + 128],
                    ternu[wpar][:, kc * 512:(kc + 1) * 512],
                    start=(kc == 0), stop=(kc == NKC - 1), skip_group_check=True),
                    reads=[xiTbufs[t], gb(f"ternu{wpar}")], writes=[bpu[par]],
                    tag=f"umm{c}_{t}_{kc}",
                    deps=fence_ac if c == 0 and t == 0 and kc == 0 else ())
            p.op("scalar", lambda e, par=par, t=t: e.activation(
                gee[par][:, :], pg[par][:, :], AF.Silu, scale=grs[:, t:t + 1]),
                reads=[bpg[par], gb("grs")], writes=[gb(f"gee{par}")],
                tag=f"silu{c}_{t}")
            p.op("scalar", lambda e, par=par, t=t: e.activation(
                uee[par][:, :], pu[par][:, :], AF.Copy, scale=urs[:, t:t + 1]),
                reads=[bpu[par], gb("urs")], writes=[gb(f"uee{par}")],
                tag=f"upsc{c}_{t}")
            p.op("vector", lambda e, par=par: e.tensor_tensor(
                gustage[par][:, :], gee[par][:, :], uee[par][:, :], ALU.mult),
                reads=[gb(f"gee{par}"), gb(f"uee{par}")],
                writes=[gb(f"gust{par}")], tag=f"gu{c}_{t}")
            p.op("vector", lambda e, par=par, t=t, c=c: e.reduce_max(
                rmax_g[:, t * 8 + c:t * 8 + c + 1], gustage[par][:, :], AX.X,
                apply_absolute_value=True),
                reads=[gb(f"gust{par}")], writes=[gb(f"rmx{t}_{c}")],
                tag=f"rmax{c}_{t}")
            dma("scalar", guv[:, t, 512 * c:512 * (c + 1)], gustage[par][:, :],
                reads=[gb(f"gust{par}")], writes=[gurow[t]], tag=f"spill{c}_{t}")
            if t == 2 and c + 1 < 8:
                emit_tern_chunk(c + 1)
            if t == 8:
                emit_wd_chunk(c)

    ffgu_bufs = ([gb(f"terng{i}") for i in range(2)] +
                 [gb(f"ternu{i}") for i in range(2)] +
                 [gb(f"gee{i}") for i in range(2)] +
                 [gb(f"uee{i}") for i in range(2)] +
                 [gb(f"gust{i}") for i in range(2)] +
                 [gb("xib"), gb("sax"), gb("grs"), gb("urs"), gb("ternd")] +
                 xiTbufs)
    fence_gu = _Prog.fence(ffgu_bufs +
                           [bpg[0], bpg[1], bpu[0], bpu[1], bpxi, bpsc])
    ff.close()
    pf.close()

    dn = ExitStack()
    sbd = lambda n, s, d: dn.enter_context(nc.sbuf_tensor(n, s, d))
    gur = sbd("gur0", [128, 2048], BF16)
    yib = sbd("yib", [128, 2048], BF16)
    yiT = [sbd(f"yiT{i}", [128, INNER], BF16) for i in range(2)]
    wdr = sbd("wdr", [128, 32 * C], BF16)

    pd_ = ExitStack()
    pdn = [pd_.enter_context(nc.psum_tensor(f"pd{i}", [128, C], F32)) for i in range(2)]
    pyT = [pd_.enter_context(nc.psum_tensor(f"pyT{i}", [128, C], BF16)) for i in range(2)]
    bpd = [gb("pd0"), gb("pd1")]
    bpyT = [gb("pyT0"), gb("pyT1")]

    for j in range(4):
        dma("gpsimd", wdr[:, :].rearrange("p (k n) -> p k n", n=C)[:, 8 * j:8 * (j + 1), :],
            wdt3[:, 8 * j:8 * (j + 1), :],
            reads=[gb("wdtd")], writes=[gb(f"wdrq{j}")], tag=f"wdrld{j}",
            deps=fence_gu if j == 0 else ())

    yv = y_d.rearrange("(t p) c -> p t c", p=128)

    def d1(t, hsel):
        par = t % 2
        p3 = t % 3
        cb = 8 + 4 * p3
        if hsel == 0:
            p.op("vector", lambda e, t=t, cb=cb: e.reduce_max(
                colt[:, cb:cb + 1], rmax_g[:, 8 * t:8 * t + 8], AX.X),
                reads=[gb(f"rmx{t}_{cc}") for cc in range(8)],
                writes=[gb(f"cqa{p3}")], tag=f"gmax{t}")
            p.op("vector", lambda e, cb=cb: e.tensor_scalar(
                colt[:, cb + 1:cb + 2], colt[:, cb:cb + 1], 1e-5, 1.0 / 127.0,
                ALU.max, ALU.mult),
                reads=[gb(f"cqa{p3}")], writes=[gb(f"cqb{p3}")], tag=f"grsgu{t}")
            p.op("vector", lambda e, cb=cb: e.reciprocal(
                colt[:, cb + 2:cb + 3], colt[:, cb + 1:cb + 2]),
                reads=[gb(f"cqb{p3}")], writes=[gb(f"cqc{p3}")], tag=f"gsgu{t}")
            p.op("vector", lambda e, cb=cb: e.tensor_scalar_mul(
                colt[:, cb + 3:cb + 4], colt[:, cb + 1:cb + 2], wcols[:, 2:3]),
                reads=[gb(f"cqb{p3}"), gb("wcols")], writes=[gb(f"cqd{p3}")],
                tag=f"gmd{t}")
        for hh in (hsel,):
            dma("scalar", gur[:, 0:1024], guv[:, t, 2048 * hh:2048 * hh + 1024],
                reads=[gurow[t]], writes=[gb("gurA")], tag=f"guldA{t}_{hh}",
                deps=fence_gu if t == 0 and hh == 0 else ())
            dma("gpsimd", gur[:, 1024:2048], guv[:, t, 2048 * hh + 1024:2048 * hh + 2048],
                reads=[gurow[t]], writes=[gb("gurB")], tag=f"guldB{t}_{hh}",
                deps=fence_gu if t == 0 and hh == 0 else ())
            p.op("vector", lambda e, cb=cb: e.tensor_scalar_mul(
                ti32[:, :], gur[:, :], colt[:, cb + 2:cb + 3]),
                reads=[gb("gurA"), gb("gurB"), gb(f"cqc{p3}")], writes=[gb("ti32q")],
                tag=f"yii{t}_{hh}",
                deps=_Prog.fence([gb("ti32")]) if t == 0 and hh == 0 else ())
            p.op("vector", lambda e: e.tensor_copy(yib[:, :], ti32[:, :]),
                 reads=[gb("ti32q")], writes=[gb("yibq")], tag=f"yib{t}_{hh}",
                 deps=fence_gu if t == 0 and hh == 0 else ())
            for g2 in range(2):
                g4 = 2 * hh + g2
                gp = g4 % 2
                for kk in range(8):
                    p.op("tensor", lambda e, kk=kk, gp=gp, g2=g2: e.transpose(
                        pyT[gp][:, 128 * kk:128 * (kk + 1)],
                        yib[:, 1024 * g2 + 128 * kk:1024 * g2 + 128 * (kk + 1)],
                        idb[:, :]),
                        reads=[gb("yibq"), gb("idb")], writes=[bpyT[gp]],
                        tag=f"yT{t}_{g4}_{kk}",
                        deps=fence_gu if t == 0 and g4 == 0 and kk == 0 else ())
                p.op("scalar", lambda e, g4=g4, gp=gp, par=par: e.copy(
                    yiT[par][:, 1024 * g4:1024 * (g4 + 1)], pyT[gp][:, :]),
                    reads=[bpyT[gp]], writes=[gb(f"yiTq{par}_{g4}")],
                    tag=f"ev_yT{t}_{g4}",
                    deps=fence_gu if t == 0 and g4 == 0 else ())

    def d2(t, hsel):
        par = t % 2
        p3 = t % 3
        cb = 8 + 4 * p3
        for kabs in range(16 * hsel, 16 * hsel + 16):
            for n in range(2):
                p.op("tensor", lambda e, kabs=kabs, n=n, par=par: e.matmul(
                    pdn[par][:, 512 * n:512 * (n + 1)],
                    yiT[par][:, 128 * kabs:128 * (kabs + 1)],
                    wdr[:, kabs * C + 512 * n:kabs * C + 512 * (n + 1)],
                    start=(kabs == 0), stop=(kabs == 31),
                    skip_group_check=True),
                    reads=[gb(f"yiTq{par}_{kabs // 8}"), gb(f"wdrq{kabs // 8}")],
                    writes=[bpd[par]], tag=f"dmm{t}_{kabs}_{n}",
                    deps=fence_gu if t == 0 and kabs == 0 and n == 0 else ())
        return

    def d2e(t):
        par = t % 2
        p3 = t % 3
        cb = 8 + 4 * p3
        p.op("vector", lambda e, par=par, cb=cb: e.tensor_scalar_mul(
            s_f[par][:, :], pdn[par][:, :], colt[:, cb + 3:cb + 4]),
            reads=[bpd[par], gb(f"cqd{p3}")], writes=[bsf[par]], tag=f"dsc{t}")
        p.op("vector", lambda e, par=par, t=t: e.tensor_tensor(
            s_g[par][:, :], s_f[par][:, :], x_sb[:, t * C:(t + 1) * C], ALU.add),
            reads=[bsf[par], xbufs[t]], writes=[bsg[par]], tag=f"res3_{t}")
        ln(s_g[par], bsg[par], s_f[par], bsf[par], eps5, gb("consts"), t, "ln3")
        dma("sync", yv[:, t, :], s_f[par][:, :],
            reads=[bsf[par]], writes=[gb("y_d")], tag=f"st_y{t}")

    d1(0, 0)
    d1(0, 1)
    d1(1, 0)
    d1(1, 1)
    for t in range(NT):
        for h in range(2):
            d2(t, h)
            if t + 2 < NT:
                d1(t + 2, h)
        d2e(t)

    p.emit(nc, stack)
    dn.close()
    pd_.close()
    return nc, stack



def _perm():
    idx = np.zeros(C, np.int64)
    for h in range(16):
        for j in range(32):
            idx[64 * h + j] = 64 * h + 2 * j
            idx[64 * h + 32 + j] = 64 * h + 2 * j + 1
    return idx


def kernel(L0_attn_x, L0_attn_Wq, L0_attn_bq, L0_attn_Wk, L0_attn_bk,
           L0_attn_Wv, L0_attn_bv, L0_attn_gamma, L0_attn_beta,
           L0_ffn_w_gate, L0_ffn_w_up, L0_ffn_w_down):
    x = np.ascontiguousarray(np.asarray(L0_attn_x, np.float32))
    Wq = np.asarray(L0_attn_Wq, np.float32)
    Wk = np.asarray(L0_attn_Wk, np.float32)
    Wv = np.asarray(L0_attn_Wv, np.float32)
    bq = np.asarray(L0_attn_bq, np.float32)
    bk = np.asarray(L0_attn_bk, np.float32)
    bv = np.asarray(L0_attn_bv, np.float32)
    gam = np.asarray(L0_attn_gamma, np.float32)
    bet = np.asarray(L0_attn_beta, np.float32)
    Wg = np.asarray(L0_ffn_w_gate, np.float32)
    Wu = np.asarray(L0_ffn_w_up, np.float32)
    Wd = np.asarray(L0_ffn_w_down, np.float32)

    P = _perm()
    wqT = np.ascontiguousarray(Wq[P, :].T)
    wkT = np.ascontiguousarray(Wk[P, :].T)
    wvT = np.ascontiguousarray(Wv.T)
    wgT = np.ascontiguousarray(Wg.T)
    wuT = np.ascontiguousarray(Wu.T)
    wdT = np.ascontiguousarray(Wd.T)
    bqb = np.ascontiguousarray(np.broadcast_to(bq[P], (128, C))).astype(ml_dtypes.bfloat16)
    bkb = np.ascontiguousarray(np.broadcast_to(bk[P], (128, C))).astype(ml_dtypes.bfloat16)
    bvb = np.ascontiguousarray(np.broadcast_to(bv, (128, C))).astype(ml_dtypes.bfloat16)
    gamb = np.ascontiguousarray(np.broadcast_to(gam, (128, C)))
    betb = np.ascontiguousarray(np.broadcast_to(bet, (128, C)))

    inv_freq = 1.0 / 10000.0 ** (np.arange(0, C, 2, dtype=np.float64) / C)
    idb = np.eye(128, dtype=ml_dtypes.bfloat16)
    onesb = np.ones((128, 1), ml_dtypes.bfloat16)
    ones1f = np.full((1, 128), 1.0 / 4096.0, np.float32)
    ones1 = np.ones((1, 128), np.float32)
    onesf = np.ones((128, 1), np.float32)
    onesrb = np.ones((1, 128), ml_dtypes.bfloat16)
    idf = np.eye(128, dtype=np.float32)
    mask = np.zeros((128, C), ml_dtypes.bfloat16)
    for c2 in range(8):
        mask[0:64, 128 * c2:128 * c2 + 64] = 1.0
        mask[64:128, 128 * c2 + 64:128 * c2 + 128] = 1.0
    consts = np.zeros((128, 8), np.float32)
    consts[:, 0] = 1e-12
    consts[:, 1] = 1e-5

    nc, stack = build_program()

    in_maps = []
    for core in range(8):
        b, hf = core // 2, core % 2
        pos = hf * T + np.arange(T, dtype=np.float64)
        slot = np.arange(C)
        fidx = 32 * (slot // 64) + (slot % 32)
        ang = pos[:, None] * inv_freq[fidx][None, :]
        cosA = np.cos(ang).astype(ml_dtypes.bfloat16)
        sinA = np.sin(ang).astype(ml_dtypes.bfloat16)
        xc = np.ascontiguousarray(x[b, hf * T:(hf + 1) * T, :])
        in_maps.append(dict(
            x=xc, xT=np.ascontiguousarray(xc.T),
            wqT=wqT, wkT=wkT, wvT=wvT, wgT=wgT, wuT=wuT, wdT=wdT,
            bqb=bqb, bkb=bkb, bvb=bvb, gamb=gamb, betb=betb,
            cosA=np.ascontiguousarray(cosA),
            sinA=np.ascontiguousarray(sinA),
            idb=idb, idf=idf, onesb=onesb, onesrb=onesrb, ones1f=ones1f,
            ones1=ones1, onesf=onesf,
            mask2h=mask, consts=consts,
        ))

    res = run_bass_kernel_spmd(nc, in_maps, core_ids=list(range(8)))
    out = np.empty((4, 4096, C), np.float32)
    for core in range(8):
        b, hf = core // 2, core % 2
        out[b, hf * T:(hf + 1) * T, :] = res.results[core]["y"]
    return out


if __name__ == "__main__":
    import importlib.util
    spec = importlib.util.spec_from_file_location("_ref", "/root/problem/reference.py")
    ref = importlib.util.module_from_spec(spec)
    spec.loader.exec_module(ref)
    ins = ref.setup_inputs()
    got = kernel(**{k: np.asarray(v) for k, v in ins.items()})
    want = np.asarray(ref.reference(**ins))
    num = ((got - want) ** 2).mean()
    den = (want ** 2).mean()
    print("var-ratio:", num / den, " maxabs:", np.abs(got - want).max())


# version 23
# speedup vs baseline: 1.0578x; 1.0578x over previous
from contextlib import ExitStack

import numpy as np
import ml_dtypes

import concourse.bass as bass
import concourse.mybir as mybir
from concourse.bass_utils import run_bass_kernel_spmd

F32 = mybir.dt.float32
BF16 = mybir.dt.bfloat16
I32 = mybir.dt.int32
AF = mybir.ActivationFunctionType
ALU = mybir.AluOpType
AX = mybir.AxisListType

T = 2048
NT = 16
C = 1024
INNER = 4096
NKC = C // 128

DEBUG_TAPS = []

ENGINES = ("tensor", "vector", "scalar", "gpsimd", "sync")


class _Buf:
    __slots__ = ("name", "writer", "readers")

    def __init__(self, name):
        self.name = name
        self.writer = None
        self.readers = []


class _Op:
    __slots__ = ("idx", "engine", "dma", "fn", "deps", "needs_inc", "inc_index",
                 "tag", "_semkey", "amount")

    def __init__(self, idx, engine, dma, fn, deps, tag):
        self.idx = idx
        self.engine = engine
        self.dma = dma
        self.fn = fn
        self.deps = deps
        self.needs_inc = False
        self.inc_index = None
        self.tag = tag
        self._semkey = None
        self.amount = 16 if dma else 1

    @property
    def semkey(self):
        return self._semkey if self.dma else self.engine


class _Prog:

    DMA_SEM_POOL = 6

    def __init__(self):
        self.ops = []

    def op(self, engine, fn, reads=(), writes=(), dma=False, deps=(), tag="",
           force_inc=False, inc_amount=None, semkey=None):
        assert engine in ENGINES
        idx = len(self.ops)
        depset = set(deps)
        for r in reads:
            if r.writer is not None:
                depset.add(r.writer)
        for w in writes:
            if w.writer is not None:
                depset.add(w.writer)
            depset.update(w.readers)
        o = _Op(idx, engine, dma, fn, depset, tag)
        o.needs_inc = bool(force_inc)
        if inc_amount is not None:
            o.amount = inc_amount
        if semkey is not None:
            o._semkey = semkey
        self.ops.append(o)
        for w in writes:
            w.writer = idx
            w.readers = []
        for r in reads:
            r.readers.append(idx)
        return idx

    @staticmethod
    def fence(bufs):
        deps = set()
        for b in bufs:
            if b.writer is not None:
                deps.add(b.writer)
            deps.update(b.readers)
        return deps

    def _filter_deps(self):
        for o in self.ops:
            kept = set()
            for d in o.deps:
                pp = self.ops[d]
                if (not pp.dma) and (not o.dma) and pp.engine == o.engine == "tensor":
                    continue
                kept.add(d)
            o.deps = kept

    def _assign_incs(self):
        for o in self.ops:
            for d in o.deps:
                self.ops[d].needs_inc = True
        for o in self.ops:
            if o.dma:
                o.needs_inc = True
        rr, totals = {}, {}
        self.prev_use_val = {}
        for o in self.ops:
            if o.dma and o._semkey is None:
                qi = rr.get(o.engine, 0)
                rr[o.engine] = qi + 1
                o._semkey = f"q{o.engine}{qi % self.DMA_SEM_POOL}"
            if o.needs_inc:
                if o.dma:
                    self.prev_use_val[o.idx] = totals.get(o.semkey, 0)
                totals[o.semkey] = totals.get(o.semkey, 0) + o.amount
                o.inc_index = totals[o.semkey]
        self.sem_totals = totals

    def emit(self, nc, stack, final_engine="sync"):
        self._filter_deps()
        self._assign_incs()
        semkeys = sorted(self.sem_totals)
        sems = {k: stack.enter_context(nc.semaphore(f"sem_{k}")) for k in semkeys}
        by_engine = {e: [] for e in ENGINES}
        for o in self.ops:
            by_engine[o.engine].append(o)
        block = stack.enter_context(nc.Block())

        def make_runner(ops, is_final):
            def run(eng):
                covered = {}
                for o in ops:
                    waits = {}
                    for d in sorted(o.deps):
                        pp = self.ops[d]
                        if waits.get(pp.semkey, 0) < pp.inc_index:
                            waits[pp.semkey] = pp.inc_index
                    if o.dma and self.prev_use_val.get(o.idx, 0) > 0:
                        v = self.prev_use_val[o.idx]
                        if waits.get(o.semkey, 0) < v:
                            waits[o.semkey] = v
                    for k, v in sorted(waits.items()):
                        if covered.get(k, 0) >= v:
                            continue
                        eng.wait_ge(sems[k], v)
                        covered[k] = v
                    ins = o.fn(eng)
                    if o.needs_inc:
                        ins.then_inc(sems[o.semkey], o.amount)
                if is_final:
                    for k in semkeys:
                        eng.wait_ge(sems[k], self.sem_totals[k])
            return run

        for e in ENGINES:
            if not by_engine[e] and e != final_engine:
                continue
            getattr(block, e)(make_runner(by_engine[e], e == final_engine))


class _APWrap:
    def __init__(self, ap):
        self._ap = ap
    def __getitem__(self, k):
        return self._ap


def _b0(ap2d, reps):
    return bass.AP(tensor=ap2d.tensor, offset=ap2d.offset,
                   ap=list(ap2d.ap) + [[0, reps]])


def build_program():
    nc = bass.Bass("TRN2", target_bir_lowering=False)
    p = _Prog()
    stack = ExitStack()

    def din(name, shape, dt=F32):
        return nc.dram_tensor(name, shape, dt, kind="ExternalInput").ap()

    x_d = din("x", [T, C])
    xT_d = din("xT", [C, T])
    wqT_d = din("wqT", [C, C])
    wkT_d = din("wkT", [C, C])
    wvT_d = din("wvT", [C, C])
    wgT_d = din("wgT", [C, INNER])
    wuT_d = din("wuT", [C, INNER])
    wdT_d = din("wdT", [INNER, C])
    bqb_d = din("bqb", [128, C], BF16)
    bkb_d = din("bkb", [128, C], BF16)
    bvb_d = din("bvb", [128, C], BF16)
    gamb_d = din("gamb", [128, C])
    betb_d = din("betb", [128, C])
    cosA_d = din("cosA", [T, C], BF16)
    sinA_d = din("sinA", [T, C], BF16)
    idb_d = din("idb", [128, 128], BF16)
    idf_d = din("idf", [128, 128])
    onesb_d = din("onesb", [128, 1], BF16)
    onesrb_d = din("onesrb", [1, 128], BF16)
    ones1f_d = din("ones1f", [1, 128])
    ones1_d = din("ones1", [1, 128])
    onesf_d = din("onesf", [128, 1])
    mask2h_d = din("mask2h", [128, C], BF16)
    consts_d = din("consts", [128, 8])

    y_d = nc.dram_tensor("y", [T, C], F32, kind="ExternalOutput").ap()
    cc_in = nc.dram_tensor("cc_in", [129, C], F32).ap()
    cc_out = nc.dram_tensor("cc_out", [129, C], F32).ap()
    gu_d = nc.dram_tensor("gu_spill", [T, INNER], BF16).ap()
    wdt_d = nc.dram_tensor("wd_tern", [INNER, C], BF16).ap()

    taps = {}
    if "x1" in DEBUG_TAPS:
        taps["x1"] = nc.dram_tensor("tap_x1", [T, C], F32, kind="ExternalOutput").ap()
    if "attn" in DEBUG_TAPS:
        taps["attn"] = nc.dram_tensor("tap_attn", [T, C], F32, kind="ExternalOutput").ap()
    if "kv" in DEBUG_TAPS:
        taps["kv"] = nc.dram_tensor("tap_kv", [129, C], F32, kind="ExternalOutput").ap()

    sbp = lambda n, s, d: stack.enter_context(nc.sbuf_tensor(n, s, d))

    x_sb = sbp("x_sb", [128, NT * C], F32)
    s_f = [sbp(f"s_f{i}", [128, C], F32) for i in range(4)]
    s_g = [sbp(f"s_g{i}", [128, C], F32) for i in range(2)]
    land = [sbp("land0", [128, 2048], F32), sbp("land1", [128, 2048], F32)]
    consts = sbp("consts_sb", [128, 8], F32)
    idb = sbp("idb_sb", [128, 128], BF16)
    idf = sbp("idf_sb", [128, 128], F32)
    onesb = sbp("onesb_sb", [128, 1], BF16)
    onesrb = sbp("onesrb_sb", [1, 128], BF16)
    ones1f = sbp("ones1f_sb", [1, 128], F32)
    ones1 = sbp("ones1_sb", [1, 128], F32)
    onesf = sbp("onesf_sb", [128, 1], F32)
    wstat = sbp("wstat_sb", [128, 64], F32)
    wcols = sbp("wcols_sb", [128, 8], F32)
    rmax_g = sbp("rmax_g_sb", [128, NT * 8], F32)
    colt = sbp("colt_sb", [128, 24], F32)
    st12 = sbp("st12_sb", [128, 24], F32)
    st2 = sbp("st2_sb", [128, 8], F32)
    nwt = sbp("nwt_sb", [128, 8], F32)

    B = {}

    def gb(name):
        if name not in B:
            B[name] = _Buf(name)
        return B[name]

    def dma(queue, dst, src, reads=(), writes=(), tag="", deps=(), inc_amount=None):
        return p.op(queue, lambda e, d=dst, s=src: e.dma_start(out=d, in_=s),
                    reads=reads, writes=writes, dma=True, tag=tag, deps=deps,
                    force_inc=True, inc_amount=inc_amount)

    eps12 = consts[:, 0:1]
    eps5 = consts[:, 1:2]

    for dst, src, nme in (
        (consts[:, :], consts_d, "consts"), (idb[:, :], idb_d, "idb"),
        (idf[:, :], idf_d, "idf"),
        (onesb[:, :], onesb_d, "onesb"), (onesrb[:, :], onesrb_d, "onesrb"),
        (ones1f[:, :], ones1f_d, "ones1f"),
        (ones1[:, :], ones1_d, "ones1"), (onesf[:, :], onesf_d, "onesf"),
    ):
        dma("gpsimd", dst, src, writes=[gb(nme)], tag="ld_" + nme)

    xv = x_d.rearrange("(t p) c -> p t c", p=128)
    xbufs = [gb(f"x{t}") for t in range(NT)]
    bsf = [gb(f"s_f{i}") for i in range(4)]
    bsg = [gb(f"s_g{i}") for i in range(2)]
    landb = [gb("land0"), gb("land1")]

    ac = ExitStack()
    sba = lambda n, s, d: ac.enter_context(nc.sbuf_tensor(n, s, d))
    xT_sb = sba("xT_sb", [128, NKC * T], BF16)
    _ct = sba("cos0", [128, C], BF16)
    _st = sba("sin0", [128, C], BF16)
    costb = [_ct, _ct]
    sintb = [_st, _st]
    mrp = [sba(f"mrp{i}", [128, C], BF16) for i in range(2)]
    rin_b = sba("rin_b", [128, C], BF16)
    l1s = ExitStack()
    sb1 = lambda n, s, d: l1s.enter_context(nc.sbuf_tensor(n, s, d))
    kvfull = sb1("kvfull_sb", [128, C], F32)
    ksrow = sb1("ksrow_sb", [1, C], F32)
    wk_sb = sb1("wk_sb", [128, NKC * C], BF16)
    wv_sb = sb1("wv_sb", [128, NKC * C], BF16)
    bkb = sb1("bkb_sb", [128, C], BF16)
    bvb = sb1("bvb_sb", [128, C], BF16)
    _knr = sb1("knr0", [128, C], BF16)
    knr_b = [_knr, _knr]
    kr_b = [sb1(f"kr{i}", [128, C], BF16) for i in range(2)]
    vs_b = [sb1(f"vs{i}", [128, C], BF16) for i in range(2)]
    wa_land = land

    for dst, src, nme in (
        (bkb[:, :], bkb_d, "bkb"),
        (bvb[:, :], bvb_d, "bvb"),
    ):
        dma("gpsimd", dst, src, writes=[gb(nme)], tag="ld_" + nme)

    for kc in range(NKC):
        lb, lt = landb[kc % 2], land[kc % 2]
        dma("sync", lt[:, :], xT_d[128 * kc:128 * (kc + 1), :],
            writes=[lb], tag=f"ld_xT{kc}")
        p.op("vector", lambda e, kc=kc, lt=lt: e.tensor_copy(
            xT_sb[:, kc * T:(kc + 1) * T], lt[:, :]),
            reads=[lb], writes=[gb(f"xT{kc}")], tag=f"cast_xT{kc}")
    xTbufs = [gb(f"xT{kc}") for kc in range(NKC)]

    for wd_, wsb, nme in ((wkT_d, wk_sb, "wk"), (wvT_d, wv_sb, "wv")):
        wvw = wd_.rearrange("(k p) n -> p k n", p=128)
        for jj in range(4):
            lb, lt = landb[jj % 2], land[jj % 2]
            dma("sync", lt[:, :].rearrange("p (k n) -> p k n", n=C),
                wvw[:, 2 * jj:2 * jj + 2, :], writes=[lb], tag=f"ld_{nme}{jj}")
            p.op("vector", lambda e, jj=jj, lt=lt, wsb=wsb: e.tensor_copy(
                wsb[:, jj * 2 * C:(jj + 1) * 2 * C], lt[:, :]),
                reads=[lb], writes=[gb(f"{nme}{jj}")], tag=f"cast_{nme}{jj}")
    wkb = [gb(f"wk{j}") for j in range(4)]
    wvb = [gb(f"wv{j}") for j in range(4)]

    for t in range(NT):
        dma("sync", x_sb[:, t * C:(t + 1) * C], xv[:, t, :],
            writes=[xbufs[t]], tag=f"ld_x{t}")

    col_of = {0: 0, 1: 16, 2: 32}
    ccount = {0: 0, 1: 0, 2: 0}
    wst_bufs = {0: [], 1: [], 2: []}
    wabs_chunks = []
    for mi, wd_ in enumerate((wgT_d, wuT_d, wdT_d)):
        w3 = wd_.rearrange("(k p) n -> p k n", p=128)
        ncol = wd_.shape[1]
        if ncol >= 2048:
            for k in range(wd_.shape[0] // 128):
                for j in range(ncol // 2048):
                    wabs_chunks.append((mi, w3[:, k:k + 1, j * 2048:(j + 1) * 2048]))
        else:
            for k in range(0, wd_.shape[0] // 128, 2):
                wabs_chunks.append((mi, w3[:, k:k + 2, :]))
    def emit_abs_chunk(ci):
        mi, ch = wabs_chunks[ci]
        lb, lt = landb[ci % 2], wa_land[ci % 2]
        dma("gpsimd", lt[:, :].rearrange("p (k n) -> p k n", n=ch.shape[2]), ch,
            writes=[lb], tag=f"wabs_ld{ci}")
        col = col_of[mi] + ccount[mi]
        ccount[mi] += 1
        wb = gb(f"wst_{mi}_{col}")
        wst_bufs[mi].append(wb)
        p.op("vector", lambda e, lt=lt, col=col: e.reduce_sum(
            wstat[:, col:col + 1], lt[:, :], AX.X, apply_absolute_value=True),
            reads=[lb], writes=[wb], tag=f"wabs_red{ci}")

    pa = ExitStack()
    pj0 = pa.enter_context(nc.psum_tensor("pj0", [128, C], F32))
    pj1 = pa.enter_context(nc.psum_tensor("pj1", [128, C], F32))
    pA = ExitStack()
    pkv = pA.enter_context(nc.psum_tensor("pkv", [128, C], F32))
    pks = pA.enter_context(nc.psum_tensor("pks", [1, C], F32))
    bpj = [gb("pj0"), gb("pj1")]
    bpkv, bpks = gb("pkv"), gb("pks")

    def bias_mm(psum, bps, biast, biasbuf, tag):
        for n in range(2):
            p.op("tensor", lambda e, n=n, psum=psum, biast=biast: e.matmul(
                psum[:, 512 * n:512 * (n + 1)], onesrb[:, :],
                biast[0:1, 512 * n:512 * (n + 1)], start=False, stop=True,
                skip_group_check=True),
                reads=[gb("onesrb"), biasbuf], writes=[bps], tag=tag + f"_{n}")

    def proj_mm(psum, bps, wsb, wbufs, t, tag):
        for kc in range(NKC):
            for n in range(2):
                p.op("tensor", lambda e, kc=kc, n=n, psum=psum, wsb=wsb, t=t:
                     e.matmul(psum[:, 512 * n:512 * (n + 1)],
                              xT_sb[:, kc * T + 128 * t:kc * T + 128 * t + 128],
                              wsb[:, kc * C + 512 * n:kc * C + 512 * (n + 1)],
                              start=(kc == 0), stop=(kc == NKC - 1),
                              skip_group_check=True),
                     reads=[xTbufs[kc], wbufs[kc // 2]], writes=[bps],
                     tag=f"{tag}_mm{kc}_{n}")

    def rope(src_ap, srcbufs, par, dst, dstbuf, tag, precast=False):
        if precast:
            p.op("vector", lambda e, src_ap=src_ap: e.tensor_copy(rin_b[:, :], src_ap),
                 reads=list(srcbufs), writes=[gb("rin_b")], tag=tag + "_pc")
            src_ap = rin_b[:, :]
            srcbufs = [gb("rin_b")]
        s3 = src_ap.rearrange("p (h half j) -> p h half j", half=2, j=32)
        d3 = dst[:, :].rearrange("p (h half j) -> p h half j", half=2, j=32)
        m0 = mrp[0][:, :]
        m1 = mrp[1][:, :]
        m0s = m0.rearrange("p (h half j) -> p h half j", half=2, j=32)
        m1s = m1.rearrange("p (h half j) -> p h half j", half=2, j=32)
        bm = [gb("mrp0"), gb("mrp1")]
        tb = [gb("cos0"), gb("sin0")]
        p.op("vector", lambda e, m0=m0, src_ap=src_ap: e.tensor_tensor(
            m0, src_ap, costb[0][:, :], ALU.mult),
            reads=list(srcbufs) + [tb[0]], writes=[bm[0]], tag=tag + "_m0")
        p.op("vector", lambda e, m1=m1, src_ap=src_ap: e.tensor_tensor(
            m1, src_ap, sintb[0][:, :], ALU.mult),
            reads=list(srcbufs) + [tb[1]], writes=[bm[1]], tag=tag + "_m1")
        p.op("vector", lambda e, d3=d3, m0s=m0s, m1s=m1s: e.tensor_tensor(
            d3[:, :, 0, :], m0s[:, :, 0, :], m1s[:, :, 1, :], ALU.subtract),
            reads=[bm[0], bm[1]], writes=[dstbuf], tag=tag + "_sub")
        p.op("vector", lambda e, d3=d3, m0s=m0s, m1s=m1s: e.tensor_tensor(
            d3[:, :, 1, :], m1s[:, :, 0, :], m0s[:, :, 1, :], ALU.add),
            reads=[bm[0], bm[1]], writes=[dstbuf], tag=tag + "_add")

    cosv = cosA_d.rearrange("(t p) i -> p t i", p=128)
    sinv = sinA_d.rearrange("(t p) i -> p t i", p=128)

    def load_tables(t, par):
        dma("gpsimd", costb[0][:, :], cosv[:, t, :],
            writes=[gb("cos0")], tag=f"ld_cos{t}")
        dma("gpsimd", sintb[0][:, :], sinv[:, t, :],
            writes=[gb("sin0")], tag=f"ld_sin{t}")

    for t in range(NT):
        par = t % 2
        load_tables(t, par)
        proj_mm(pj0, bpj[0], wk_sb, wkb, t, f"k{t}")
        bias_mm(pj0, bpj[0], bkb, gb("bkb"), f"kbias{t}")
        p.op("scalar", lambda e, par=par: e.activation(
            s_g[par][:, :], pj0[:, :], AF.Gelu),
            reads=[bpj[0]], writes=[bsg[par]], tag=f"kgelu{t}")
        p.op("vector", lambda e, par=par: e.tensor_scalar_add(
            knr_b[par][:, :], s_g[par][:, :], 0.21),
            reads=[bsg[par]], writes=[gb("knr0")], tag=f"k021_{t}")
        for n in range(2):
            p.op("tensor", lambda e, n=n, t=t, par=par: e.matmul(
                pks[0:1, 512 * n:512 * (n + 1)], onesb[:, :],
                knr_b[par][:, 512 * n:512 * (n + 1)],
                start=(t == 0), stop=(t == NT - 1), skip_group_check=True),
                reads=[gb("onesb"), gb("knr0")], writes=[bpks],
                tag=f"ksum{t}_{n}")
        rope(knr_b[par][:, :], [gb("knr0")], par, kr_b[par],
             gb(f"kr{par}"), f"ropek{t}")
        proj_mm(pj1, bpj[1], wv_sb, wvb, t, f"v{t}")
        bias_mm(pj1, bpj[1], bvb, gb("bvb"), f"vbias{t}")
        p.op("scalar", lambda e, par=par: e.activation(
            vs_b[par][:, :], pj1[:, :], AF.Copy, scale=2.0 ** -12),
            reads=[bpj[1]], writes=[gb(f"vs{par}")], tag=f"vsc{t}")
        for c2 in range(NKC):
            p.op("tensor", lambda e, c2=c2, t=t, par=par: e.matmul(
                pkv[:, 128 * c2:128 * (c2 + 1)],
                kr_b[par][:, 128 * c2:128 * (c2 + 1)],
                vs_b[par][:, 128 * c2:128 * (c2 + 1)],
                start=(t == 0), stop=(t == NT - 1), skip_group_check=True),
                reads=[gb(f"kr{par}"), gb(f"vs{par}")], writes=[bpkv],
                tag=f"kv{t}_{c2}")
        for ci in range(3 * t, 3 * t + 3):
            emit_abs_chunk(ci)

    p.op("vector", lambda e: e.tensor_copy(kvfull[:, :], pkv[:, :]),
         reads=[bpkv], writes=[gb("kvfull")], tag="ev_kv")
    p.op("vector", lambda e: e.tensor_copy(ksrow[:, :], pks[:, :]),
         reads=[bpks], writes=[gb("ksrow")], tag="ev_ks")
    dma("gpsimd", cc_in[0:128, :], kvfull[:, :], reads=[gb("kvfull")],
        writes=[gb("ccin")], tag="cc_st_kv")
    dma("gpsimd", cc_in[128:129, :], ksrow[:, :], reads=[gb("ksrow")],
        writes=[gb("ccin")], tag="cc_st_ks")
    p.op("gpsimd", lambda e: e.collective_compute(
        "AllReduce", ALU.add, replica_groups=[[0, 1], [2, 3], [4, 5], [6, 7]],
        ins=[cc_in], outs=[cc_out]),
        reads=[gb("ccin")], writes=[gb("ccout")], dma=True, inc_amount=1,
        tag="cc", semkey="qcc")
    if "kv" in DEBUG_TAPS:
        dma("gpsimd", taps["kv"], cc_out, reads=[gb("ccout")],
            writes=[gb("tapkv")], tag="tap_kv")

    fence_l1 = _Prog.fence([gb("kvfull"), gb("ksrow")] +
                           [gb(f"wk{j}") for j in range(4)] +
                           [gb(f"wv{j}") for j in range(4)] +
                           [gb("bkb"), gb("bvb")] +
                           [gb("knr0")] +
                           [gb(f"kr{i}") for i in range(2)] +
                           [gb(f"vs{i}") for i in range(2)])
    l1s.close()
    l2s = ExitStack()
    sb2 = lambda n, s, d: l2s.enter_context(nc.sbuf_tensor(n, s, d))
    wq_sb = sb2("wq_sb", [128, NKC * C], BF16)
    bqb = sb2("bqb_sb", [128, C], BF16)
    gamb = sb2("gamb_sb", [128, C], F32)
    betb = sb2("betb_sb", [128, C], F32)
    mask2h = sb2("mask2h_sb", [128, C], BF16)
    kvf2 = sb2("kvf2_sb", [128, C], F32)
    ksr2 = sb2("ksr2_sb", [1, C], F32)
    kv_b = sb2("kv_b_sb", [128, C], BF16)
    kmean = sb2("kmean_sb", [128, C], F32)
    qr_b = [sb2(f"qr{i}", [128, C], BF16) for i in range(2)]
    qrT_s = [sb2(f"qrT{i}", [128, C], BF16) for i in range(2)]
    z_s = [sb2(f"z{i}", [128, 16], F32) for i in range(2)]
    for dst, src, nme in ((bqb[:, :], bqb_d, "bqb"),
                          (gamb[:, :], gamb_d, "gamb"),
                          (betb[:, :], betb_d, "betb"),
                          (mask2h[:, :], mask2h_d, "mask2h")):
        dma("gpsimd", dst, src, writes=[gb(nme)], tag="ld_" + nme,
            deps=fence_l1)
    dma("gpsimd", kvf2[:, :], cc_out[0:128, :], reads=[gb("ccout")],
        writes=[gb("kvf2")], tag="cc_ld_kv", deps=fence_l1)
    dma("gpsimd", ksr2[:, :], cc_out[128:129, :], reads=[gb("ccout")],
        writes=[gb("ksr2")], tag="cc_ld_ks", deps=fence_l1)
    p.op("vector", lambda e: e.tensor_tensor(kv_b[:, :], kvf2[:, :],
                                             mask2h[:, :], ALU.mult),
         reads=[gb("kvf2"), gb("mask2h")], writes=[gb("kv_b")], tag="kv_mask")
    for n in range(2):
        p.op("tensor", lambda e, n=n: e.matmul(
            pj0[:, 512 * n:512 * (n + 1)], ones1f[:, :],
            ksr2[0:1, 512 * n:512 * (n + 1)], start=True, stop=True),
            reads=[gb("ones1f"), gb("ksr2")], writes=[bpj[0]], tag=f"kmean{n}")
    p.op("vector", lambda e: e.tensor_copy(kmean[:, :], pj0[:, :]),
         reads=[bpj[0]], writes=[gb("kmean")], tag="ev_kmean")
    wvw = wqT_d.rearrange("(k p) n -> p k n", p=128)
    for jj in range(4):
        lb, lt = landb[jj % 2], land[jj % 2]
        dma("sync", lt[:, :].rearrange("p (k n) -> p k n", n=C),
            wvw[:, 2 * jj:2 * jj + 2, :], writes=[lb], tag=f"ld_wq{jj}",
            deps=fence_l1)
        p.op("vector", lambda e, jj=jj, lt=lt: e.tensor_copy(
            wq_sb[:, jj * 2 * C:(jj + 1) * 2 * C], lt[:, :]),
            reads=[lb], writes=[gb(f"wq{jj}")], tag=f"cast_wq{jj}")
    wqb = [gb(f"wq{j}") for j in range(4)]

    fence_b = _Prog.fence([bpkv, bpks])
    pA.close()
    pC = ExitStack()
    pqT = [pC.enter_context(nc.psum_tensor(f"pqT{i}", [128, C], BF16))
           for i in range(2)]
    pout = pC.enter_context(nc.psum_tensor("pout", [128, C], F32))
    bpqT = [gb("pqT0"), gb("pqT1")]
    bpout = gb("pout")

    def _rsqrt_newton(vcol, vbuf, ocol, obuf, eps, nb, tag):
        ve = nwt[:, nb:nb + 1]
        y = nwt[:, nb + 1:nb + 2]
        aa = nwt[:, nb + 2:nb + 3]
        cc = nwt[:, nb + 3:nb + 4]
        bnw = gb(f"nwt{nb}")
        p.op("vector", lambda e: e.tensor_scalar_add(ve, vcol, eps),
             reads=[vbuf], writes=[bnw], tag=tag + "_ve")
        p.op("vector", lambda e: e.tensor_scalar(
            y.bitcast(I32), ve.bitcast(I32), 1, None, ALU.arith_shift_right),
            reads=[bnw], writes=[bnw], tag=tag + "_sh")
        p.op("vector", lambda e: e.tensor_scalar(
            y.bitcast(I32), y.bitcast(I32), -1, 0x5F3759DF, ALU.mult, ALU.add),
            reads=[bnw], writes=[bnw], tag=tag + "_sd")
        for it in range(3):
            p.op("vector", lambda e: e.tensor_tensor(aa, y, y, ALU.mult),
                 reads=[bnw], writes=[bnw], tag=tag + f"_a{it}")
            p.op("vector", lambda e: e.tensor_tensor(aa, aa, ve, ALU.mult),
                 reads=[bnw], writes=[bnw], tag=tag + f"_b{it}")
            p.op("vector", lambda e: e.tensor_scalar(cc, aa, -0.5, 1.5,
                                                     ALU.mult, ALU.add),
                 reads=[bnw], writes=[bnw], tag=tag + f"_c{it}")
            if it < 2:
                p.op("vector", lambda e: e.tensor_tensor(y, cc, y, ALU.mult),
                     reads=[bnw], writes=[bnw], tag=tag + f"_y{it}")
            else:
                p.op("vector", lambda e, ocol=ocol: e.tensor_tensor(
                    ocol, cc, y, ALU.mult),
                    reads=[bnw], writes=[obuf], tag=tag + f"_y{it}")

    def ln(src, srcbuf, dstslot, dstbuf, epscol, epsbuf, t, nme):
        o = 12 * (t % 2)
        o2 = 4 * (t % 2)
        p.op("vector", lambda e, src=src, o=o: e.bn_stats(
            st12[:, o:o + 6], src[:, 0:512]),
            reads=[srcbuf], writes=[gb("st12")], tag=f"{nme}_bn0_{t}")
        p.op("vector", lambda e, src=src, o=o: e.bn_stats(
            st12[:, o + 6:o + 12], src[:, 512:1024]),
            reads=[srcbuf], writes=[gb("st12")], tag=f"{nme}_bn1_{t}")
        p.op("vector", lambda e, o=o, o2=o2: e.bn_aggr(
            st2[:, o2:o2 + 2], st12[:, o:o + 12]),
            reads=[gb("st12")], writes=[gb("st2")], tag=f"{nme}_ag_{t}")
        p.op("scalar", lambda e, o2=o2, epscol=epscol: e.activation(
            colt[:, o2:o2 + 1], st2[:, o2 + 1:o2 + 2], AF.Sqrt, bias=epscol),
            reads=[gb("st2"), epsbuf], writes=[gb("colt")], tag=f"{nme}_sq_{t}")
        p.op("vector", lambda e, o2=o2: e.reciprocal(
            colt[:, o2 + 1:o2 + 2], colt[:, o2:o2 + 1]),
            reads=[gb("colt")], writes=[gb("colt")], tag=f"{nme}_rc_{t}")
        p.op("vector", lambda e, src=src, dstslot=dstslot, o2=o2: e.tensor_scalar(
            dstslot[:, :], src[:, :], st2[:, o2:o2 + 1],
            colt[:, o2 + 1:o2 + 2], ALU.subtract, ALU.mult),
            reads=[srcbuf, gb("st2"), gb("colt")], writes=[dstbuf],
            tag=f"{nme}_nm_{t}")

    pouts = [pj1, pout]
    bpouts = [bpj[1], bpout]

    def l2_s1(t):
        par = t % 2
        load_tables(t, par)
        proj_mm(pj0, bpj[0], wq_sb, wqb, t, f"q{t}")
        bias_mm(pj0, bpj[0], bqb, gb("bqb"), f"qbias{t}")
        p.op("scalar", lambda e, par=par: e.activation(
            s_g[par][:, :], pj0[:, :], AF.Gelu),
            reads=[bpj[0]], writes=[bsg[par]], tag=f"qgelu{t}")
        p.op("scalar", lambda e, par=par: e.activation(
            s_f[2 + par][:, :], s_g[par][:, :], AF.Copy, bias=0.21),
            reads=[bsg[par]], writes=[bsf[2 + par]], tag=f"q021_{t}")
        rope(s_f[2 + par][:, :], [bsf[2 + par]], par, qr_b[par],
             gb(f"qr{par}"), f"ropeq{t}", precast=True)
        for kc in range(NKC):
            p.op("tensor", lambda e, kc=kc, par=par: e.transpose(
                pqT[par][:, 128 * kc:128 * (kc + 1)],
                qr_b[par][:, 128 * kc:128 * (kc + 1)], idb[:, :]),
                reads=[gb(f"qr{par}"), gb("idb")], writes=[bpqT[par]],
                tag=f"qT{t}_{kc}", deps=fence_b if t == 0 and kc == 0 else ())
        p.op("scalar", lambda e, par=par: e.copy(
            qrT_s[par][:, :], pqT[par][:, :]),
            reads=[bpqT[par]], writes=[gb(f"qrT{par}")], tag=f"ev_qT{t}")
        for c2 in range(NKC):
            p.op("tensor", lambda e, c2=c2, par=par: e.matmul(
                pouts[par][:, 128 * c2:128 * (c2 + 1)],
                qrT_s[par][:, 128 * c2:128 * (c2 + 1)],
                kv_b[:, 128 * c2:128 * (c2 + 1)], start=True, stop=True),
                reads=[gb(f"qrT{par}"), gb("kv_b")], writes=[bpouts[par]],
                tag=f"omm{t}_{c2}", deps=fence_b if t == 0 and c2 == 0 else ())

    def l2_s2(t):
        par = t % 2
        p.op("vector", lambda e, par=par: e.tensor_tensor(
            s_g[par][:, :], s_f[2 + par][:, :], kmean[:, :], ALU.mult),
            reads=[bsf[2 + par], gb("kmean")], writes=[bsg[par]], tag=f"zprod{t}")
        p.op("vector", lambda e, par=par: e.reduce_sum(
            z_s[par][:, 0:16],
            s_g[par][:, :].rearrange("p (h d) -> p h d", d=64), AX.X),
            reads=[bsg[par]], writes=[gb(f"z{par}")], tag=f"zred{t}")
        p.op("vector", lambda e, par=par: e.tensor_scalar_add(
            z_s[par][:, 0:16], z_s[par][:, 0:16], 1e-6),
            reads=[gb(f"z{par}")], writes=[gb(f"z{par}")], tag=f"zeps{t}")
        p.op("vector", lambda e, par=par: e.reciprocal(
            z_s[par][:, 0:16], z_s[par][:, 0:16]),
            reads=[gb(f"z{par}")], writes=[gb(f"z{par}")], tag=f"zrec{t}")
        p.op("vector", lambda e, par=par: e.tensor_tensor(
            s_f[par][:, :].rearrange("p (h d) -> p h d", d=64),
            pouts[par][:, :].rearrange("p (h d) -> p h d", d=64),
            _b0(z_s[par][:, 0:16], 64), ALU.mult),
            reads=[bpouts[par], gb(f"z{par}")], writes=[bsf[par]], tag=f"oz{t}")
        if "attn" in DEBUG_TAPS:
            dma("sync", taps["attn"].rearrange("(t p) c -> p t c", p=128)[:, t, :],
                s_f[par][:, :], reads=[bsf[par]], writes=[gb("tapattn")],
                tag=f"tap_attn{t}")
        p.op("vector", lambda e, par=par, t=t: e.tensor_tensor(
            s_g[par][:, :], s_f[par][:, :], x_sb[:, t * C:(t + 1) * C], ALU.add),
            reads=[bsf[par], xbufs[t]], writes=[bsg[par]], tag=f"res1_{t}")
        ln(s_g[par], bsg[par], s_f[par], bsf[par], eps12, gb("consts"), t, "ln1")
        p.op("vector", lambda e, par=par, t=t: e.tensor_tensor(
            s_f[2 + par][:, :], x_sb[:, t * C:(t + 1) * C], betb[:, :], ALU.add),
            reads=[xbufs[t], gb("betb")], writes=[bsf[2 + par]], tag=f"xb{t}")
        p.op("vector", lambda e, par=par: e.tensor_tensor(
            s_f[par][:, :], s_f[par][:, :], gamb[:, :], ALU.mult),
            reads=[bsf[par], gb("gamb")], writes=[bsf[par]], tag=f"gam{t}")
        p.op("vector", lambda e, par=par: e.tensor_tensor(
            s_g[par][:, :], s_f[par][:, :], s_f[2 + par][:, :], ALU.add),
            reads=[bsf[par], bsf[2 + par]], writes=[bsg[par]], tag=f"res2_{t}")
        ln2dst = bass.AP(tensor=x_sb[:, :].tensor, offset=t * C,
                         ap=[[NT * C, 128], [1, C]])
        ln(s_g[par], bsg[par], _APWrap(ln2dst), xbufs[t], eps5, gb("consts"), t, "ln2")

    l2_s1(0)
    for t in range(NT):
        if t + 1 < NT:
            l2_s1(t + 1)
        l2_s2(t)

    if "x1" in DEBUG_TAPS:
        for t in range(NT):
            dma("sync", taps["x1"].rearrange("(t p) c -> p t c", p=128)[:, t, :],
                x_sb[:, t * C:(t + 1) * C], reads=[xbufs[t]],
                writes=[gb("tapx1")], tag=f"tap_x1_{t}")

    attn_bufs = (xTbufs + wkb + wvb + wqb +
                 [gb(n) for n in ("bqb", "bkb", "bvb", "gamb", "betb",
                                  "mask2h", "kvfull", "kv_b", "ksrow",
                                  "kmean", "wa0", "wa1")] +
                 [gb("cos0"), gb("sin0")] +
                 [gb(f"knr{i}") for i in range(2)] +
                 [gb(f"kr{i}") for i in range(2)] +
                 [gb(f"vs{i}") for i in range(2)] +
                 [gb(f"qr{par}"), gb(f"qrT{par}")] +
                 [gb(f"z{i}") for i in range(2)] +
                 [gb(f"mrp{i}") for i in range(2)])
    fence_ac = _Prog.fence(attn_bufs + [bpj[0], bpj[1], bpqT[0], bpqT[1], bpout])
    l2s.close()
    ac.close()
    pC.close()
    pa.close()

    ti32 = stack.enter_context(nc.sbuf_tensor("ti32", [128, 2048], I32))
    ff = ExitStack()
    sbf = lambda n, s, d: ff.enter_context(nc.sbuf_tensor(n, s, d))
    xiT_sb = sbf("xiT_sb", [128, NKC * T], BF16)
    terng = [sbf(f"terng{i}", [128, INNER], BF16) for i in range(2)]
    ternu = [sbf(f"ternu{i}", [128, INNER], BF16) for i in range(2)]
    ternd = sbf("ternd", [128, 2048], BF16)
    gee = [sbf(f"gee{i}", [128, 512], F32) for i in range(2)]
    uee = [sbf(f"uee{i}", [128, 512], F32) for i in range(2)]
    gustage = [sbf(f"gust{i}", [128, 512], BF16) for i in range(2)]
    sax = sbf("sax", [128, NT], F32)
    rsax = sbf("rsax", [128, NT], F32)
    grs = sbf("grs", [128, NT], F32)
    urs = sbf("urs", [128, NT], F32)
    xib = sbf("xib", [128, C], BF16)

    pf = ExitStack()
    pg = [pf.enter_context(nc.psum_tensor(f"pg{i}", [128, 512], F32)) for i in range(3)]
    pu = [pf.enter_context(nc.psum_tensor(f"pu{i}", [128, 512], F32)) for i in range(3)]
    pxi = pf.enter_context(nc.psum_tensor("pxi", [128, C], BF16))
    psc = pf.enter_context(nc.psum_tensor("psc", [128, 512], F32))
    bpg = [gb("pg0"), gb("pg1"), gb("pg2")]
    bpu = [gb("pu0"), gb("pu1"), gb("pu2")]
    bpxi, bpsc = gb("pxi"), gb("psc")

    for mi in range(3):
        cw = col_of[mi]
        nch = ccount[mi]
        p.op("vector", lambda e, cw=cw, nch=nch, mi=mi: e.reduce_sum(
            wstat[:, 48 + mi:49 + mi], wstat[:, cw:cw + nch], AX.X),
            reads=wst_bufs[mi], writes=[gb(f"wrs{mi}")], tag=f"wrs{mi}")
        p.op("tensor", lambda e, mi=mi: e.matmul(
            psc[0:1, mi:mi + 1], onesf[:, :], wstat[:, 48 + mi:49 + mi],
            start=True, stop=True),
            reads=[gb("onesf"), gb(f"wrs{mi}")], writes=[bpsc],
            tag=f"wS{mi}", deps=fence_ac if mi == 0 else ())
    p.op("vector", lambda e: e.tensor_scalar(
        wstat[0:1, 52:55], psc[0:1, 0:3], 1.0 / (C * INNER), 1e-5,
        ALU.mult, ALU.max),
        reads=[bpsc], writes=[gb("wmu")], tag="wmean")
    p.op("vector", lambda e: e.reciprocal(wstat[0:1, 56:59], wstat[0:1, 52:55]),
         reads=[gb("wmu")], writes=[gb("wsw")], tag="wsw")
    p.op("tensor", lambda e: e.matmul(psc[:, 8:11], ones1[:, :],
                                      wstat[0:1, 52:55], start=True, stop=True),
         reads=[gb("ones1"), gb("wmu")], writes=[bpsc], tag="bc_m")
    p.op("tensor", lambda e: e.matmul(psc[:, 11:14], ones1[:, :],
                                      wstat[0:1, 56:59], start=True, stop=True),
         reads=[gb("ones1"), gb("wsw")], writes=[bpsc], tag="bc_sw")
    p.op("vector", lambda e: e.tensor_copy(wcols[:, 0:6], psc[:, 8:14]),
         reads=[bpsc], writes=[gb("wcols")], tag="ev_wcols")
    swcol = {0: wcols[:, 3:4], 1: wcols[:, 4:5], 2: wcols[:, 5:6]}

    for t in range(NT):
        par = t % 2
        xt = x_sb[:, t * C:(t + 1) * C]
        p.op("vector", lambda e, xt=xt, par=par: e.reduce_max(
            colt[:, 20 + par:21 + par], xt, AX.X, apply_absolute_value=True),
            reads=[xbufs[t]], writes=[gb(f"coltq{par}")], tag=f"xmax{t}",
            deps=fence_ac if t == 0 else ())
        p.op("vector", lambda e, t=t, par=par: e.tensor_scalar(
            rsax[:, t:t + 1], colt[:, 20 + par:21 + par], 1e-5, 1.0 / 127.0,
            ALU.max, ALU.mult),
            reads=[gb(f"coltq{par}")], writes=[gb("rsax")], tag=f"rsa{t}")
        p.op("vector", lambda e, t=t: e.reciprocal(
            sax[:, t:t + 1], rsax[:, t:t + 1]),
            reads=[gb("rsax")], writes=[gb("sax")], tag=f"sa{t}")
        p.op("vector", lambda e, xt=xt, t=t: e.tensor_scalar_mul(
            ti32[:, 0:C], xt, sax[:, t:t + 1]),
            reads=[xbufs[t], gb("sax")], writes=[gb("ti32")], tag=f"xii{t}",
            deps=fence_ac if t == 0 else ())
        p.op("scalar", lambda e: e.copy(xib[:, :], ti32[:, 0:C]),
             reads=[gb("ti32")], writes=[gb("xib")], tag=f"xib{t}")
        for kc in range(NKC):
            p.op("tensor", lambda e, kc=kc: e.transpose(
                pxi[:, 128 * kc:128 * (kc + 1)],
                xib[:, 128 * kc:128 * (kc + 1)], idb[:, :]),
                reads=[gb("xib"), gb("idb")], writes=[bpxi], tag=f"xiT{t}_{kc}",
                deps=fence_ac if t == 0 and kc == 0 else ())
        p.op("scalar", lambda e, t=t: e.copy(
            xiT_sb[:, :].rearrange("p (k tt) -> p k tt", tt=T)[:, :, 128 * t:128 * (t + 1)],
            pxi[:, :].rearrange("p (k tt) -> p k tt", tt=128)),
            reads=[bpxi], writes=[gb(f"xiTc{t}")], tag=f"ev_xiT{t}")
    xiTbufs = [gb(f"xiTc{t}") for t in range(NT)]
    p.op("vector", lambda e: e.tensor_scalar_mul(grs[:, :], rsax[:, :],
                                                 wcols[:, 0:1]),
         reads=[gb("rsax"), gb("wcols")], writes=[gb("grs")], tag="grs")
    p.op("vector", lambda e: e.tensor_scalar_mul(urs[:, :], rsax[:, :],
                                                 wcols[:, 1:2]),
         reads=[gb("rsax"), gb("wcols")], writes=[gb("urs")], tag="urs")

    w3g = wgT_d.rearrange("(k p) n -> p k n", p=128)
    w3u = wuT_d.rearrange("(k p) n -> p k n", p=128)
    guv = gu_d.rearrange("(t p) n -> p t n", p=128)
    gurow = [gb(f"gurow{t}") for t in range(NT)]

    w3d = wdT_d.rearrange("(k p) n -> p k n", p=128)
    wdt3 = wdt_d.rearrange("(k p) n -> p k n", p=128)

    def emit_wd_chunk(j):
        for half in range(2):
            lb, lt = landb[half], land[half]
            dma("sync", lt[:, :].rearrange("p (k n) -> p k n", n=C),
                w3d[:, 4 * j + 2 * half:4 * j + 2 * half + 2, :],
                writes=[lb], tag=f"wdld{j}_{half}")
            p.op("vector", lambda e, lt=lt: e.tensor_scalar_mul(
                ti32[:, :], lt[:, :], swcol[2]),
                reads=[lb, gb("wcols")], writes=[gb("ti32")],
                tag=f"wdsc{j}_{half}")
            p.op("scalar", lambda e: e.copy(ternd[:, :], ti32[:, :]),
                 reads=[gb("ti32")], writes=[gb("ternd")], tag=f"wdb{j}_{half}")
            p.op("vector", lambda e: e.tensor_scalar(
                ternd[:, :], ternd[:, :], -1.0, 1.0, ALU.max, ALU.min),
                reads=[gb("ternd")], writes=[gb("ternd")], tag=f"wdcl{j}_{half}")
            dma("gpsimd", wdt3[:, 4 * j + 2 * half:4 * j + 2 * half + 2, :],
                ternd[:, :].rearrange("p (k n) -> p k n", n=C),
                reads=[gb("ternd")], writes=[gb("wdtd")], tag=f"wdst{j}_{half}")

    def emit_tern_chunk(c):
        wpar = c % 2
        for (w3, tern, nme, mi) in ((w3g, terng, "g", 0), (w3u, ternu, "u", 1)):
            tb = gb(f"tern{nme}{wpar}")
            for half in range(2):
                lb, lt = landb[half], land[half]
                dma("sync", lt[:, :].rearrange("p (k n) -> p k n", n=512),
                    w3[:, 4 * half:4 * half + 4, 512 * c:512 * (c + 1)],
                    writes=[lb], tag=f"w{nme}ld{c}_{half}")
                p.op("vector", lambda e, lt=lt, mi=mi: e.tensor_scalar_mul(
                    ti32[:, :], lt[:, :], swcol[mi]),
                    reads=[lb, gb("wcols")], writes=[gb("ti32")],
                    tag=f"w{nme}sc{c}_{half}")
                p.op("scalar", lambda e, tern=tern, wpar=wpar, half=half: e.copy(
                    tern[wpar][:, 2048 * half:2048 * (half + 1)], ti32[:, :]),
                    reads=[gb("ti32")], writes=[tb], tag=f"w{nme}b{c}_{half}")
            p.op("vector", lambda e, tern=tern, wpar=wpar: e.tensor_scalar(
                tern[wpar][:, :], tern[wpar][:, :], -1.0, 1.0, ALU.max, ALU.min),
                reads=[tb], writes=[tb], tag=f"w{nme}cl{c}")

    emit_tern_chunk(0)
    for c in range(8):
        wpar = c % 2
        for t in range(NT):
            par = t % 2
            pp = (c * NT + t) % 3
            for kc in range(NKC):
                p.op("tensor", lambda e, kc=kc, t=t, wpar=wpar, pp=pp: e.matmul(
                    pg[pp][:, :],
                    xiT_sb[:, kc * T + 128 * t:kc * T + 128 * t + 128],
                    terng[wpar][:, kc * 512:(kc + 1) * 512],
                    start=(kc == 0), stop=(kc == NKC - 1), skip_group_check=True),
                    reads=[xiTbufs[t], gb(f"terng{wpar}")], writes=[bpg[pp]],
                    tag=f"gmm{c}_{t}_{kc}",
                    deps=fence_ac if c == 0 and t == 0 and kc == 0 else ())
            for kc in range(NKC):
                p.op("tensor", lambda e, kc=kc, t=t, wpar=wpar, pp=pp: e.matmul(
                    pu[pp][:, :],
                    xiT_sb[:, kc * T + 128 * t:kc * T + 128 * t + 128],
                    ternu[wpar][:, kc * 512:(kc + 1) * 512],
                    start=(kc == 0), stop=(kc == NKC - 1), skip_group_check=True),
                    reads=[xiTbufs[t], gb(f"ternu{wpar}")], writes=[bpu[pp]],
                    tag=f"umm{c}_{t}_{kc}",
                    deps=fence_ac if c == 0 and t == 0 and kc == 0 else ())
            p.op("scalar", lambda e, par=par, t=t, pp=pp: e.activation(
                gee[par][:, :], pg[pp][:, :], AF.Silu, scale=grs[:, t:t + 1]),
                reads=[bpg[pp], gb("grs")], writes=[gb(f"gee{par}")],
                tag=f"silu{c}_{t}")
            p.op("scalar", lambda e, par=par, t=t, pp=pp: e.activation(
                uee[par][:, :], pu[pp][:, :], AF.Copy, scale=urs[:, t:t + 1]),
                reads=[bpu[pp], gb("urs")], writes=[gb(f"uee{par}")],
                tag=f"upsc{c}_{t}")
            p.op("vector", lambda e, par=par: e.tensor_tensor(
                gustage[par][:, :], gee[par][:, :], uee[par][:, :], ALU.mult),
                reads=[gb(f"gee{par}"), gb(f"uee{par}")],
                writes=[gb(f"gust{par}")], tag=f"gu{c}_{t}")
            p.op("vector", lambda e, par=par, t=t, c=c: e.reduce_max(
                rmax_g[:, t * 8 + c:t * 8 + c + 1], gustage[par][:, :], AX.X,
                apply_absolute_value=True),
                reads=[gb(f"gust{par}")], writes=[gb(f"rmx{t}_{c}")],
                tag=f"rmax{c}_{t}")
            dma("scalar", guv[:, t, 512 * c:512 * (c + 1)], gustage[par][:, :],
                reads=[gb(f"gust{par}")], writes=[gurow[t]], tag=f"spill{c}_{t}")
            if t == 2 and c + 1 < 8:
                emit_tern_chunk(c + 1)
            if t == 8:
                emit_wd_chunk(c)

    ffgu_bufs = ([gb(f"terng{i}") for i in range(2)] +
                 [gb(f"ternu{i}") for i in range(2)] +
                 [gb(f"gee{i}") for i in range(2)] +
                 [gb(f"uee{i}") for i in range(2)] +
                 [gb(f"gust{i}") for i in range(2)] +
                 [gb("xib"), gb("sax"), gb("grs"), gb("urs"), gb("ternd")] +
                 xiTbufs)
    fence_gu = _Prog.fence(ffgu_bufs + bpg + bpu + [bpxi, bpsc])
    ff.close()
    pf.close()

    dn = ExitStack()
    sbd = lambda n, s, d: dn.enter_context(nc.sbuf_tensor(n, s, d))
    gur = sbd("gur0", [128, 2048], BF16)
    yib = sbd("yib", [128, 2048], BF16)
    yiT = [sbd(f"yiT{i}", [128, INNER], BF16) for i in range(2)]
    wdr = sbd("wdr", [128, 32 * C], BF16)

    pd_ = ExitStack()
    pdn = [pd_.enter_context(nc.psum_tensor(f"pd{i}", [128, C], F32)) for i in range(2)]
    pyT = [pd_.enter_context(nc.psum_tensor(f"pyT{i}", [128, C], BF16)) for i in range(2)]
    bpd = [gb("pd0"), gb("pd1")]
    bpyT = [gb("pyT0"), gb("pyT1")]

    for j in range(4):
        dma("gpsimd", wdr[:, :].rearrange("p (k n) -> p k n", n=C)[:, 8 * j:8 * (j + 1), :],
            wdt3[:, 8 * j:8 * (j + 1), :],
            reads=[gb("wdtd")], writes=[gb(f"wdrq{j}")], tag=f"wdrld{j}",
            deps=fence_gu if j == 0 else ())

    yv = y_d.rearrange("(t p) c -> p t c", p=128)

    def d1(t, hsel):
        par = t % 2
        p3 = t % 3
        cb = 8 + 4 * p3
        if hsel == 0:
            p.op("vector", lambda e, t=t, cb=cb: e.reduce_max(
                colt[:, cb:cb + 1], rmax_g[:, 8 * t:8 * t + 8], AX.X),
                reads=[gb(f"rmx{t}_{cc}") for cc in range(8)],
                writes=[gb(f"cqa{p3}")], tag=f"gmax{t}")
            p.op("vector", lambda e, cb=cb: e.tensor_scalar(
                colt[:, cb + 1:cb + 2], colt[:, cb:cb + 1], 1e-5, 1.0 / 127.0,
                ALU.max, ALU.mult),
                reads=[gb(f"cqa{p3}")], writes=[gb(f"cqb{p3}")], tag=f"grsgu{t}")
            p.op("vector", lambda e, cb=cb: e.reciprocal(
                colt[:, cb + 2:cb + 3], colt[:, cb + 1:cb + 2]),
                reads=[gb(f"cqb{p3}")], writes=[gb(f"cqc{p3}")], tag=f"gsgu{t}")
            p.op("vector", lambda e, cb=cb: e.tensor_scalar_mul(
                colt[:, cb + 3:cb + 4], colt[:, cb + 1:cb + 2], wcols[:, 2:3]),
                reads=[gb(f"cqb{p3}"), gb("wcols")], writes=[gb(f"cqd{p3}")],
                tag=f"gmd{t}")
        for hh in (hsel,):
            dma("scalar", gur[:, 0:1024], guv[:, t, 2048 * hh:2048 * hh + 1024],
                reads=[gurow[t]], writes=[gb("gurA")], tag=f"guldA{t}_{hh}",
                deps=fence_gu if t == 0 and hh == 0 else ())
            dma("gpsimd", gur[:, 1024:2048], guv[:, t, 2048 * hh + 1024:2048 * hh + 2048],
                reads=[gurow[t]], writes=[gb("gurB")], tag=f"guldB{t}_{hh}",
                deps=fence_gu if t == 0 and hh == 0 else ())
            p.op("vector", lambda e, cb=cb: e.tensor_scalar_mul(
                ti32[:, :], gur[:, :], colt[:, cb + 2:cb + 3]),
                reads=[gb("gurA"), gb("gurB"), gb(f"cqc{p3}")], writes=[gb("ti32q")],
                tag=f"yii{t}_{hh}",
                deps=_Prog.fence([gb("ti32")]) if t == 0 and hh == 0 else ())
            p.op("vector", lambda e: e.tensor_copy(yib[:, :], ti32[:, :]),
                 reads=[gb("ti32q")], writes=[gb("yibq")], tag=f"yib{t}_{hh}",
                 deps=fence_gu if t == 0 and hh == 0 else ())
            for g2 in range(2):
                g4 = 2 * hh + g2
                gp = g4 % 2
                for kk in range(8):
                    p.op("tensor", lambda e, kk=kk, gp=gp, g2=g2: e.transpose(
                        pyT[gp][:, 128 * kk:128 * (kk + 1)],
                        yib[:, 1024 * g2 + 128 * kk:1024 * g2 + 128 * (kk + 1)],
                        idb[:, :]),
                        reads=[gb("yibq"), gb("idb")], writes=[bpyT[gp]],
                        tag=f"yT{t}_{g4}_{kk}",
                        deps=fence_gu if t == 0 and g4 == 0 and kk == 0 else ())
                p.op("scalar", lambda e, g4=g4, gp=gp, par=par: e.copy(
                    yiT[par][:, 1024 * g4:1024 * (g4 + 1)], pyT[gp][:, :]),
                    reads=[bpyT[gp]], writes=[gb(f"yiTq{par}_{g4}")],
                    tag=f"ev_yT{t}_{g4}",
                    deps=fence_gu if t == 0 and g4 == 0 else ())

    def d2(t, hsel):
        par = t % 2
        p3 = t % 3
        cb = 8 + 4 * p3
        for kabs in range(16 * hsel, 16 * hsel + 16):
            for n in range(2):
                p.op("tensor", lambda e, kabs=kabs, n=n, par=par: e.matmul(
                    pdn[par][:, 512 * n:512 * (n + 1)],
                    yiT[par][:, 128 * kabs:128 * (kabs + 1)],
                    wdr[:, kabs * C + 512 * n:kabs * C + 512 * (n + 1)],
                    start=(kabs == 0), stop=(kabs == 31),
                    skip_group_check=True),
                    reads=[gb(f"yiTq{par}_{kabs // 8}"), gb(f"wdrq{kabs // 8}")],
                    writes=[bpd[par]], tag=f"dmm{t}_{kabs}_{n}",
                    deps=fence_gu if t == 0 and kabs == 0 and n == 0 else ())
        return

    def d2e(t):
        par = t % 2
        p3 = t % 3
        cb = 8 + 4 * p3
        p.op("vector", lambda e, par=par, cb=cb: e.tensor_scalar_mul(
            s_f[par][:, :], pdn[par][:, :], colt[:, cb + 3:cb + 4]),
            reads=[bpd[par], gb(f"cqd{p3}")], writes=[bsf[par]], tag=f"dsc{t}")
        p.op("vector", lambda e, par=par, t=t: e.tensor_tensor(
            s_g[par][:, :], s_f[par][:, :], x_sb[:, t * C:(t + 1) * C], ALU.add),
            reads=[bsf[par], xbufs[t]], writes=[bsg[par]], tag=f"res3_{t}")
        ln(s_g[par], bsg[par], s_f[par], bsf[par], eps5, gb("consts"), t, "ln3")
        dma("sync", yv[:, t, :], s_f[par][:, :],
            reads=[bsf[par]], writes=[gb("y_d")], tag=f"st_y{t}")

    d1(0, 0)
    d1(0, 1)
    d1(1, 0)
    d1(1, 1)
    for t in range(NT):
        for h in range(2):
            d2(t, h)
            if t + 2 < NT:
                d1(t + 2, h)
        d2e(t)

    p.emit(nc, stack)
    dn.close()
    pd_.close()
    return nc, stack



def _perm():
    idx = np.zeros(C, np.int64)
    for h in range(16):
        for j in range(32):
            idx[64 * h + j] = 64 * h + 2 * j
            idx[64 * h + 32 + j] = 64 * h + 2 * j + 1
    return idx


def kernel(L0_attn_x, L0_attn_Wq, L0_attn_bq, L0_attn_Wk, L0_attn_bk,
           L0_attn_Wv, L0_attn_bv, L0_attn_gamma, L0_attn_beta,
           L0_ffn_w_gate, L0_ffn_w_up, L0_ffn_w_down):
    x = np.ascontiguousarray(np.asarray(L0_attn_x, np.float32))
    Wq = np.asarray(L0_attn_Wq, np.float32)
    Wk = np.asarray(L0_attn_Wk, np.float32)
    Wv = np.asarray(L0_attn_Wv, np.float32)
    bq = np.asarray(L0_attn_bq, np.float32)
    bk = np.asarray(L0_attn_bk, np.float32)
    bv = np.asarray(L0_attn_bv, np.float32)
    gam = np.asarray(L0_attn_gamma, np.float32)
    bet = np.asarray(L0_attn_beta, np.float32)
    Wg = np.asarray(L0_ffn_w_gate, np.float32)
    Wu = np.asarray(L0_ffn_w_up, np.float32)
    Wd = np.asarray(L0_ffn_w_down, np.float32)

    P = _perm()
    wqT = np.ascontiguousarray(Wq[P, :].T)
    wkT = np.ascontiguousarray(Wk[P, :].T)
    wvT = np.ascontiguousarray(Wv.T)
    wgT = np.ascontiguousarray(Wg.T)
    wuT = np.ascontiguousarray(Wu.T)
    wdT = np.ascontiguousarray(Wd.T)
    bqb = np.ascontiguousarray(np.broadcast_to(bq[P], (128, C))).astype(ml_dtypes.bfloat16)
    bkb = np.ascontiguousarray(np.broadcast_to(bk[P], (128, C))).astype(ml_dtypes.bfloat16)
    bvb = np.ascontiguousarray(np.broadcast_to(bv, (128, C))).astype(ml_dtypes.bfloat16)
    gamb = np.ascontiguousarray(np.broadcast_to(gam, (128, C)))
    betb = np.ascontiguousarray(np.broadcast_to(bet, (128, C)))

    inv_freq = 1.0 / 10000.0 ** (np.arange(0, C, 2, dtype=np.float64) / C)
    idb = np.eye(128, dtype=ml_dtypes.bfloat16)
    onesb = np.ones((128, 1), ml_dtypes.bfloat16)
    ones1f = np.full((1, 128), 1.0 / 4096.0, np.float32)
    ones1 = np.ones((1, 128), np.float32)
    onesf = np.ones((128, 1), np.float32)
    onesrb = np.ones((1, 128), ml_dtypes.bfloat16)
    idf = np.eye(128, dtype=np.float32)
    mask = np.zeros((128, C), ml_dtypes.bfloat16)
    for c2 in range(8):
        mask[0:64, 128 * c2:128 * c2 + 64] = 1.0
        mask[64:128, 128 * c2 + 64:128 * c2 + 128] = 1.0
    consts = np.zeros((128, 8), np.float32)
    consts[:, 0] = 1e-12
    consts[:, 1] = 1e-5
    consts[:, 2] = 0.21

    nc, stack = build_program()

    in_maps = []
    for core in range(8):
        b, hf = core // 2, core % 2
        pos = hf * T + np.arange(T, dtype=np.float64)
        slot = np.arange(C)
        fidx = 32 * (slot // 64) + (slot % 32)
        ang = pos[:, None] * inv_freq[fidx][None, :]
        cosA = np.cos(ang).astype(ml_dtypes.bfloat16)
        sinA = np.sin(ang).astype(ml_dtypes.bfloat16)
        xc = np.ascontiguousarray(x[b, hf * T:(hf + 1) * T, :])
        in_maps.append(dict(
            x=xc, xT=np.ascontiguousarray(xc.T),
            wqT=wqT, wkT=wkT, wvT=wvT, wgT=wgT, wuT=wuT, wdT=wdT,
            bqb=bqb, bkb=bkb, bvb=bvb, gamb=gamb, betb=betb,
            cosA=np.ascontiguousarray(cosA),
            sinA=np.ascontiguousarray(sinA),
            idb=idb, idf=idf, onesb=onesb, onesrb=onesrb, ones1f=ones1f,
            ones1=ones1, onesf=onesf,
            mask2h=mask, consts=consts,
        ))

    res = run_bass_kernel_spmd(nc, in_maps, core_ids=list(range(8)))
    out = np.empty((4, 4096, C), np.float32)
    for core in range(8):
        b, hf = core // 2, core % 2
        out[b, hf * T:(hf + 1) * T, :] = res.results[core]["y"]
    return out


if __name__ == "__main__":
    import importlib.util
    spec = importlib.util.spec_from_file_location("_ref", "/root/problem/reference.py")
    ref = importlib.util.module_from_spec(spec)
    spec.loader.exec_module(ref)
    ins = ref.setup_inputs()
    got = kernel(**{k: np.asarray(v) for k, v in ins.items()})
    want = np.asarray(ref.reference(**ins))
    num = ((got - want) ** 2).mean()
    den = (want ** 2).mean()
    print("var-ratio:", num / den, " maxabs:", np.abs(got - want).max())
